# version 07
# baseline (speedup 1.0000x reference)
import contextlib
import numpy as np
import ml_dtypes
import concourse.bass as bass
import concourse.mybir as mybir
from concourse.bass_utils import run_bass_kernel_spmd
from concourse.expressions_rust import RuntimeValue

F32 = mybir.dt.float32
BF16 = mybir.dt.bfloat16
I32 = mybir.dt.int32
AF = mybir.ActivationFunctionType
ALU = mybir.AluOpType
AX = mybir.AxisListType

B, L, H = 4, 8192, 1024
D2, N, RNK = 2048, 16, 64
DH = D2 // 2
NDBLK = DH // 128
NKC = H // 128
THALF = L // 2
TC = 1024
NTC = L // TC
TCP1 = 512
NTCP1 = L // TCP1
TC2 = 512
NTC2 = L // TC2
AP1 = TCP1 // 128
NTC3 = THALF // TC
EPS = 1e-5


class Prog:

    def __init__(self, nc, blk):
        self.nc = nc
        self.blk = blk
        self._sems = {}

    def sem(self, name):
        if name not in self._sems:
            cm = self.nc.semaphore(name)
            h = cm.__enter__()
            self._sems[name] = [h, 0, cm]
        return self._sems[name]

    def inc(self, name, instr, n):
        s = self.sem(name)
        instr.then_inc(s[0], n)
        s[1] += n
        return s[1]

    def val(self, name):
        return self.sem(name)[1]

    def tval(self, name):
        return (name, self.sem(name)[1])

    def wait(self, eng, name, v):
        eng.wait_ge(self.sem(name)[0], v)

    def wait2(self, eng, tv):
        eng.wait_ge(self.sem(tv[0])[0], tv[1])


def build(debug_taps=()):
    nc = bass.Bass("TRN2", target_bir_lowering=False,
                   detect_race_conditions=False)

    def din(name, shape, dt=F32):
        return nc.dram_tensor(name, shape, dt, kind="ExternalInput")

    x_h = din("x", [L, H])
    xo_h = din("x_own", [THALF, H])
    win_h = din("win_t", [H, 2048])
    wout_h = din("wout_t", [DH, H])
    wx_h = din("wx_t", [DH, 128])
    wdt_h = din("wdt_t", [RNK, DH])
    convw_h = din("conv_w", [128, NDBLK * 4])
    convb_h = din("conv_b", [128, NDBLK])
    bdt_h = din("b_dt", [128, NDBLK])
    a_h = din("a_log", [128, NDBLK * N])
    c_h = din("c_row", [1, H])
    wmod_h = din("wmod", [H, 6 * H])
    cnw1_h = din("cn_w1", [H, H])
    cnb1_h = din("cn_b1", [128, 8])
    cnw2_h = din("cn_w2", [H, 512])
    cnb2_h = din("cn_b2", [128, 4])
    cnw3_h = din("cn_w3", [512, 256])
    cnb3_h = din("cn_b3", [128, 2])
    cnw4_h = din("cn_w4", [128, 2])
    cnb4_h = din("cn_b4", [1, 1])
    gcw1_h = din("gc_w1", [H, 512])
    gcb1_h = din("gc_b1", [128, 4])
    gcw2_h = din("gc_w2", [512, 256])
    gcb2_h = din("gc_b2", [128, 2])
    gcw3_h = din("gc_w3", [128, 2])
    gcb3_h = din("gc_b3", [1, 1])
    ar_h = din("ar", [1, 1])
    meta_h = din("meta", [1, 8], I32)

    out_h = nc.dram_tensor("out", [THALF, H], F32, kind="ExternalOutput")

    xc_sp = nc.dram_tensor("xc_sp", [NDBLK, 128, L], BF16)
    sr_sp = nc.dram_tensor("sr_sp", [NDBLK, 128, L], BF16)
    bc_sp = nc.dram_tensor("bc_sp", [2 * N, L], BF16)
    mod_sp = nc.dram_tensor("mod_sp", [6 * H], F32)
    x1_sp = nc.dram_tensor("x1_sp", [THALF, H], F32)
    h2_sp = nc.dram_tensor("h2_sp", [NKC, 128, THALF], BF16)
    gcv_sp = nc.dram_tensor("gcv_sp", [H], F32)

    xdbl_part = nc.dram_tensor("xdbl_part", [128, L], F32)
    attn_stage = nc.dram_tensor("attn_stage", [NKC, 128, L], F32)
    attn_in = nc.dram_tensor("attn_in", [2, NKC, 128, THALF], F32)
    sh_xdbl = nc.dram_tensor("sh_xdbl", [2, 128, L], F32, addr_space="Shared")
    sh_attn = nc.dram_tensor("sh_attn", [2, NKC, 128, L], F32, addr_space="Shared")
    sh_sum = nc.dram_tensor("sh_sum", [2, 1, H], F32, addr_space="Shared")
    sh_flag = nc.dram_tensor("sh_flag", [2, 1], I32, addr_space="Shared")

    taps = {}

    def tap(name, shape, dt=F32):
        if name in debug_taps:
            taps[name] = nc.dram_tensor("tap_" + name, shape, dt,
                                        kind="ExternalOutput")
            return taps[name]
        return None

    AL = lambda nm, sh, dt: nc.alloc_sbuf_tensor(nm, sh, dt).ap()
    meta = AL("meta_s", [1, 8], I32)
    wout_bf = AL("wout_bf", [128, NDBLK * H], BF16)
    wx_bf = AL("wx_bf", [128, NDBLK * 128], BF16)
    wdt_bf = AL("wdt_bf", [64, DH], BF16)
    convw = AL("convw_s", [128, NDBLK * 4], F32)
    convb = AL("convb_s", [128, NDBLK], F32)
    bdt = AL("bdt_s", [128, NDBLK], F32)
    a_sb = AL("a_sb", [128, NDBLK * N], F32)
    ident_bf = AL("ident_bf", [128, 128], BF16)
    ident_f32 = AL("ident_f32", [128, 128], F32)
    ones_bf = AL("ones_bf", [128, 1], BF16)
    ones_row = AL("ones_row", [1, 128], BF16)
    state = AL("state_s", [128, NDBLK * N], F32)
    xdbl_bf = AL("xdbl_bf", [128, L], BF16)
    flag_src = AL("flag_src", [1, 1], I32)
    flag_rd = AL("flag_rd", [1, 1], I32)
    fence_rd = AL("fence_rd", [1, 4], F32)

    consts_sb = AL("consts_sb", [128, 4], F32)
    CONST_VALS = [1e-5, 1.0, -0.5, -1.0]
    consts_h = nc.inline_tensor(
        np.tile(np.array(CONST_VALS, np.float32), (128, 1)), "consts_c")
    for ci, cv in enumerate(CONST_VALS):
        nc.const_aps.register(F32, cv, consts_sb[:, ci:ci + 1])

    id_np = np.eye(128)
    id_h = nc.inline_tensor(id_np.astype(ml_dtypes.bfloat16), "id_bf_c")
    idf_h = nc.inline_tensor(id_np.astype(np.float32), "id_f32_c")
    ones_h = nc.inline_tensor(np.ones((128, 1), dtype=ml_dtypes.bfloat16), "ones_c")
    onesr_h = nc.inline_tensor(np.ones((1, 128), dtype=ml_dtypes.bfloat16), "onesr_c")

    blk_cm = nc.Block()
    blk = blk_cm.__enter__()
    P = Prog(nc, blk)

    @blk.sync
    def _(e):
        for dst, src in [(meta, meta_h.ap()), (convw, convw_h.ap()),
                         (convb, convb_h.ap()), (bdt, bdt_h.ap()),
                         (a_sb, a_h.ap())]:
            P.inc("d01", e.dma_start(out=dst, in_=src), 16)
        P.inc("d01", e.dma_start(out=consts_sb, in_=consts_h.ap()), 16)
        P.inc("d01", e.dma_start(out=ident_bf, in_=id_h.ap().bitcast(BF16)), 16)
        P.inc("d01", e.dma_start(out=ident_f32, in_=idf_h.ap()), 16)
        P.inc("d01", e.dma_start(out=ones_bf, in_=ones_h.ap().bitcast(BF16)), 16)
        P.inc("d01", e.dma_start(out=ones_row, in_=onesr_h.ap().bitcast(BF16)), 16)
    t_const = P.tval("d01")

    regs = {}
    xfer = {}

    @blk.gpsimd
    def _(e):
        P.wait2(e, t_const)
        r1 = e.alloc_register("g_rp_keep")
        e.load(r1, meta[0:1, 0:1])
        regs["g_rp"] = RuntimeValue(r1, min_val=0, max_val=1)
        r2 = e.alloc_register("g_rq_keep")
        e.load(r2, meta[0:1, 1:2])
        regs["g_rq"] = RuntimeValue(r2, min_val=0, max_val=1)

    @blk.scalar
    def _(e):
        P.wait2(e, t_const)
        r5 = e.alloc_register("a_rp_keep")
        e.load(r5, meta[0:1, 0:1])
        regs["a_rp"] = RuntimeValue(r5, min_val=0, max_val=1)

    @blk.sync
    def _(e):
        P.wait2(e, t_const)
        r3 = e.alloc_register("s_rq_keep")
        e.load(r3, meta[0:1, 1:2])
        regs["s_rq"] = RuntimeValue(r3, min_val=0, max_val=1)
        r4 = e.alloc_register("s_rt_keep")
        e.load(r4, meta[0:1, 5:6])
        regs["s_rt"] = RuntimeValue(r4, min_val=0, max_val=THALF)

    @blk.scalar
    def _(e):
        P.wait2(e, t_const)
        P.inc("act", e.activation(a_sb, a_sb, AF.Exp), 1)
    t_aexp = P.val("act")

    @blk.vector
    def _(e):
        P.wait(e, "act", t_aexp)
        P.inc("dve", e.tensor_scalar_mul(a_sb, a_sb, -1.0), 1)
    t_aneg = P.val("dve")

    _st = contextlib.ExitStack()
    stg32_t = _st.enter_context(nc.sbuf_tensor("stg32", [128, NKC * 2048], F32))
    silc_t = _st.enter_context(nc.sbuf_tensor("silc", [1, H], F32))
    czero_t = _st.enter_context(nc.sbuf_tensor("czero", [128, H], F32))
    cbf_t = _st.enter_context(nc.sbuf_tensor("cbf", [128, NKC], BF16))
    wmcbf_t = _st.enter_context(nc.sbuf_tensor("wmcbf", [128, 768 * NKC], BF16))
    modsb_t = _st.enter_context(nc.sbuf_tensor("modsb", [128, 48], F32))
    modtr_t = _st.enter_context(nc.sbuf_tensor("modtr", [48, 128], F32))
    gtmp_t = _st.enter_context(nc.sbuf_tensor("gtmp", [128, 16], F32))
    gsg_t = _st.enter_context(nc.sbuf_tensor("gsg", [128, 16], F32))
    p0a_t = _st.enter_context(nc.psum_tensor("p0a", [128, 128], F32))
    p0d_t = _st.enter_context(nc.psum_tensor("p0d", [128, 48], F32))
    if True:
        stg32 = stg32_t.ap(); silc = silc_t.ap(); czero = czero_t.ap()
        cbf = cbf_t.ap(); wmcbf = wmcbf_t.ap(); modsb = modsb_t.ap()
        modtr = modtr_t.ap(); gtmp = gtmp_t.ap(); gsg = gsg_t.ap()
        p0a = p0a_t.ap(); p0d = p0d_t.ap()

        @blk.sync
        def _(e):
            P.inc("d07", e.dma_start(out=silc, in_=c_h.ap()), 16)
        t_win32 = P.tval("d07")

        @blk.sync
        def _(e):
            P.inc("d08", e.dma_start(
                out=stg32[:, 0:NDBLK * H].rearrange("p (d o) -> p d o", d=NDBLK),
                in_=wout_h.ap().rearrange("(d p) o -> p d o", p=128)), 16)
            P.inc("d08", e.dma_start(
                out=stg32[:, NDBLK * H:NDBLK * H + NDBLK * 128].rearrange("p (d o) -> p d o", d=NDBLK),
                in_=wx_h.ap().rearrange("(d p) o -> p d o", p=128)), 16)
            P.inc("d08", e.dma_start(
                out=stg32[0:64, NDBLK * H + NDBLK * 128:
                          NDBLK * H + NDBLK * 128 + DH],
                in_=wdt_h.ap()), 16)
        t_wincast = P.tval("d08")
        t_w32b = P.tval("d08")

        @blk.vector
        def _(e):
            P.wait2(e, t_w32b)
            P.inc("dve", e.tensor_copy(wout_bf, stg32[:, 0:NDBLK * H]), 1)
            P.inc("dve", e.tensor_copy(
                wx_bf, stg32[:, NDBLK * H:NDBLK * H + NDBLK * 128]), 1)
            P.inc("dve", e.tensor_copy(
                wdt_bf, stg32[0:64, NDBLK * H + NDBLK * 128:
                              NDBLK * H + NDBLK * 128 + DH]), 1)
        t_wcast = P.val("dve")

        @blk.scalar
        def _(e):
            P.wait2(e, t_win32)
            P.inc("act", e.activation(silc, silc, AF.Silu), 1)
        t_silc = P.val("act")

        @blk.vector
        def _(e):
            P.inc("dve", e.memset(czero, 0.0), 1)
            P.wait(e, "act", t_silc)
            P.inc("dve", e.tensor_copy(czero[0:1, :], silc), 1)
        t_cz = P.val("dve")

        for kc in range(NKC):
            @blk.tensor
            def _(e, kc=kc):
                if kc == 0:
                    P.wait(e, "dve", t_cz)
                    P.wait2(e, t_const)
                else:
                    P.wait(e, "dve", P.val("dve"))
                P.inc("mm", e.transpose(
                    p0a, czero[:, kc * 128:(kc + 1) * 128], ident_f32), 1)
            t_tp = P.val("mm")

            @blk.vector
            def _(e, kc=kc, t_tp=t_tp):
                P.wait(e, "mm", t_tp)
                P.inc("dve", e.tensor_copy(cbf[:, kc:kc + 1], p0a[:, 0:1]), 1)
        t_cbf = P.val("dve")

        for cc in range(8):
            @blk.sync
            def _(e, cc=cc):
                if cc == 0:
                    P.wait(e, "dve", t_wcast)
                else:
                    P.wait(e, "dve", P.val("dve"))
                P.inc("d11", e.dma_start(
                    out=stg32[:, 0:NKC * 768].rearrange("p (kc o) -> p kc o", kc=NKC),
                    in_=wmod_h.ap()[:, cc * 768:(cc + 1) * 768]
                        .rearrange("(kc p) o -> p kc o", p=128)), 16)
            t_wm = P.tval("d11")

            @blk.vector
            def _(e, t_wm=t_wm, cc=cc):
                P.wait2(e, t_wm)
                if cc > 0:
                    P.wait(e, "mm", P.val("mm"))
                P.inc("dve", e.tensor_copy(wmcbf, stg32[:, 0:NKC * 768]), 1)
            t_wmb = P.val("dve")

            @blk.tensor
            def _(e, cc=cc, t_wmb=t_wmb):
                P.wait(e, "dve", t_wmb)
                if cc == 0:
                    P.wait(e, "dve", t_cbf)
                for ob in range(6):
                    oc = cc * 6 + ob
                    for kc in range(NKC):
                        P.inc("mm", e.matmul(
                            p0d[:, oc:oc + 1],
                            wmcbf[:, kc * 768 + ob * 128:kc * 768 + (ob + 1) * 128],
                            cbf[:, kc:kc + 1],
                            start=(kc == 0), stop=(kc == NKC - 1)), 1)
        t_mod = P.val("mm")

        @blk.vector
        def _(e):
            P.wait(e, "mm", t_mod)
            P.inc("dve", e.tensor_copy(modsb, p0d), 1)
        t_modsb = P.val("dve")

        @blk.scalar
        def _(e):
            P.wait(e, "dve", t_modsb)
            P.inc("act", e.activation(gsg[:, 0:8], modsb[:, 16:24], AF.Sign), 1)
            P.inc("act", e.activation(gsg[:, 8:16], modsb[:, 40:48], AF.Sign), 1)
            P.inc("act", e.activation(gtmp[:, 0:8], modsb[:, 16:24], AF.Abs), 1)
            P.inc("act", e.activation(gtmp[:, 8:16], modsb[:, 40:48], AF.Abs), 1)
        t_g1 = P.val("act")

        @blk.vector
        def _(e):
            P.wait(e, "act", t_g1)
            P.inc("dve", e.tensor_scalar_max(gtmp, gtmp, 0.25), 1)
            P.wait(e, "dve", P.val("dve"))
            P.inc("dve", e.tensor_mul(modsb[:, 16:24], gsg[:, 0:8], gtmp[:, 0:8]), 1)
            P.inc("dve", e.tensor_mul(modsb[:, 40:48], gsg[:, 8:16], gtmp[:, 8:16]), 1)
            P.inc("dve", e.tensor_scalar_add(modsb[:, 8:16], modsb[:, 8:16], 1.0), 1)
            P.inc("dve", e.tensor_scalar_add(modsb[:, 32:40], modsb[:, 32:40], 1.0), 1)
        t_g2 = P.val("dve")

        @blk.tensor
        def _(e):
            P.wait(e, "dve", t_g2)
            P.inc("mm", e.transpose(p0a[0:48, 0:128], modsb, ident_f32), 1)
        t_mtp = P.val("mm")

        @blk.vector
        def _(e):
            P.wait(e, "mm", t_mtp)
            P.inc("dve", e.tensor_copy(modtr, p0a[0:48, 0:128]), 1)
        t_mtr = P.val("dve")

        @blk.gpsimd
        def _(e):
            P.wait(e, "dve", t_mtr)
            P.inc("d12", e.dma_start(
                out=mod_sp.ap().rearrange("(c p) -> c p", p=128), in_=modtr), 16)
        t_modsp = P.tval("d12")

    _st.close()

    sa_rep = AL("sa_rep", [128, H], F32)
    ca1_rep = AL("ca1_rep", [128, H], F32)

    @blk.sync
    def _(e):
        P.wait2(e, t_modsp)
        P.inc("d13", e.dma_start(
            out=sa_rep, in_=mod_sp.ap()[0:H].unsqueeze(0).partition_broadcast(128)), 16)
        P.inc("d13", e.dma_start(
            out=ca1_rep, in_=mod_sp.ap()[H:2 * H].unsqueeze(0).partition_broadcast(128)), 16)
    t_modrep = P.tval("d13")

    def _phase1():
      _st = contextlib.ExitStack()
      xin_t = _st.enter_context(nc.sbuf_tensor("xin", [128, 2 * AP1 * H], F32))
      winbf_t = _st.enter_context(nc.sbuf_tensor("win_bf", [128, NKC * 2048], BF16))
      lnsc_t = _st.enter_context(nc.sbuf_tensor("lnsc", [128, 16], F32))
      sq1_t = _st.enter_context(nc.sbuf_tensor("sq1", [128, H], F32))
      hmod_t = _st.enter_context(nc.sbuf_tensor("hmod", [128, AP1 * H], BF16))
      hT_t = _st.enter_context(nc.sbuf_tensor("hT", [128, NKC * TCP1], BF16))
      xs1_t = _st.enter_context(nc.sbuf_tensor("xs1", [128, NDBLK * (TCP1 + 4)], BF16))
      xcc_t = _st.enter_context(nc.sbuf_tensor("xcc", [128, NDBLK * TCP1], BF16))
      src1_t = _st.enter_context(nc.sbuf_tensor("src1", [128, NDBLK * TCP1], BF16))
      xdp_t = _st.enter_context(nc.sbuf_tensor("xdp", [128, TCP1], F32))
      ptp_t = _st.enter_context(nc.psum_tensor("ptp", [128, 128], BF16))
      pwa_t = _st.enter_context(nc.psum_tensor("pwa", [128, TCP1], F32))
      pwb_t = _st.enter_context(nc.psum_tensor("pwb", [128, TCP1], F32))
      pxd_t = _st.enter_context(nc.psum_tensor("pxd", [128, TCP1], F32))
      if True:
          xin = xin_t.ap(); win_bf = winbf_t.ap()
          lnsc = lnsc_t.ap(); sq1 = sq1_t.ap()
          hmod = hmod_t.ap(); hT = hT_t.ap(); xs1 = xs1_t.ap()
          xcc = xcc_t.ap(); src1 = src1_t.ap(); xdp = xdp_t.ap()
          ptp = ptp_t.ap(); pwa = pwa_t.ap(); pwb = pwb_t.ap(); pxd = pxd_t.ap()

          for wh in range(2):
              @blk.sync
              def _(e, wh=wh):
                  if wh == 0:
                      P.wait2(e, t_modrep)
                  else:
                      P.wait(e, "dve", P.val("dve"))
                  P.inc("dwin", e.dma_start(
                      out=xin.rearrange("p (kc o) -> p kc o", kc=NKC // 2),
                      in_=win_h.ap()[wh * (H // 2):(wh + 1) * (H // 2), :]
                      .rearrange("(kc p) o -> p kc o", p=128)), 16)
              t_wl = P.tval("dwin")

              @blk.vector
              def _(e, wh=wh, t_wl=t_wl):
                  P.wait2(e, t_wl)
                  P.inc("dve", e.tensor_copy(
                      win_bf[:, wh * (NKC // 2) * 2048:(wh + 1) * (NKC // 2) * 2048],
                      xin[:, 0:(NKC // 2) * 2048]), 1)
          t_winld = P.val("dve")

          @blk.vector
          def _(e):
              P.wait(e, "dve", t_winld)
              for db in range(NDBLK):
                  P.inc("dve", e.memset(xs1[:, db * (TCP1 + 4):db * (TCP1 + 4) + 4], 0.0), 1)
          t_halo0 = P.val("dve")

          xin_free = {}
          hT_free = {}
          xd_free = {}
          xdp_spilled = {}

          for tc in range(NTCP1):
              xb = xin[:, (tc % 2) * AP1 * H:((tc % 2) + 1) * AP1 * H]

              @blk.sync
              def _(e, tc=tc, xb=xb):
                  if tc == 0:
                      P.wait(e, "dve", t_winld)
                  if tc >= 2:
                      P.wait(e, "dve", xin_free[tc - 2])
                  P.inc("d15", e.dma_start(
                      out=xb.rearrange("p (a h) -> p a h", a=AP1),
                      in_=x_h.ap()[tc * TCP1:(tc + 1) * TCP1, :]
                      .rearrange("(a p) h -> p a h", p=128)), 16)
              t_x = P.tval("d15")

              for a in range(AP1):
                  xt = xb[:, a * H:(a + 1) * H]

                  @blk.vector
                  def _(e, xt=xt, a=a, t_x=t_x):
                      if a == 0:
                          P.wait2(e, t_x)
                      P.inc("dve", e.reduce_sum(lnsc[:, 0:1], xt, axis=AX.X), 1)
                  t_rs = P.val("dve")

                  @blk.scalar
                  def _(e, xt=xt, a=a, t_x=t_x, t_rs=t_rs):
                      if a == 0:
                          P.wait2(e, t_x)
                      P.wait(e, "dve", t_rs)
                      P.inc("act", e.activation(sq1, xt, AF.Square,
                                                accum_out=lnsc[:, 1:2]), 1)
                  t_sq = P.val("act")

                  @blk.vector
                  def _(e, t_sq=t_sq):
                      P.wait(e, "act", t_sq)
                      P.inc("dve", e.tensor_scalar_mul(lnsc[:, 2:3], lnsc[:, 0:1], 1.0 / H), 1)
                      P.inc("dve", e.tensor_scalar_mul(lnsc[:, 3:4], lnsc[:, 1:2], 1.0 / H), 1)
                      P.wait(e, "dve", P.val("dve"))
                      P.inc("dve", e.tensor_mul(lnsc[:, 4:5], lnsc[:, 2:3], lnsc[:, 2:3]), 1)
                      P.wait(e, "dve", P.val("dve"))
                      P.inc("dve", e.tensor_sub(lnsc[:, 5:6], lnsc[:, 3:4], lnsc[:, 4:5]), 1)
                  t_var = P.val("dve")

                  @blk.scalar
                  def _(e, t_var=t_var):
                      P.wait(e, "dve", t_var)
                      P.inc("act", e.activation(lnsc[:, 6:7], lnsc[:, 5:6], AF.Ln, bias=EPS), 1)
                      P.inc("act", e.activation(lnsc[:, 7:8], lnsc[:, 6:7], AF.Exp, scale=-0.5), 1)
                  t_rstd = P.val("act")

                  @blk.vector
                  def _(e, xt=xt, a=a, t_rstd=t_rstd, tc=tc):
                      P.wait(e, "act", t_rstd)
                      if a == 0 and tc > 0:
                          P.wait(e, "mm", hmod_mm_done[tc - 1])
                      P.inc("dve", e.tensor_mul(lnsc[:, 8:9], lnsc[:, 2:3], lnsc[:, 7:8]), 1)
                      P.wait(e, "dve", P.val("dve"))
                      P.inc("dve", e.tensor_scalar(sq1, xt, lnsc[:, 7:8], lnsc[:, 8:9],
                                                   ALU.mult, ALU.subtract), 1)
                      P.wait(e, "dve", P.val("dve"))
                      P.inc("dve", e.tensor_mul(sq1, sq1, ca1_rep), 1)
                      P.wait(e, "dve", P.val("dve"))
                      P.inc("dve", e.tensor_add(hmod[:, a * H:(a + 1) * H], sq1, sa_rep), 1)
              t_hmod = P.val("dve")
              xin_free[tc] = t_hmod

              for a in range(AP1):
                  for kc in range(NKC):
                      @blk.tensor
                      def _(e, a=a, kc=kc, t_hmod=t_hmod, tc=tc):
                          if a == 0 and kc == 0:
                              P.wait(e, "dve", t_hmod)
                              if tc > 0:
                                  P.wait(e, "mm", hT_free[tc - 1])
                          else:
                              P.wait(e, "dve", P.val("dve"))
                          P.inc("mm", e.transpose(
                              ptp, hmod[:, a * H + kc * 128:a * H + (kc + 1) * 128],
                              ident_bf), 1)
                      t_tp = P.val("mm")

                      @blk.vector
                      def _(e, a=a, kc=kc, t_tp=t_tp):
                          P.wait(e, "mm", t_tp)
                          P.inc("dve", e.tensor_copy(
                              hT[:, kc * TCP1 + a * 128:kc * TCP1 + (a + 1) * 128], ptp), 1)
              t_hT = P.val("dve")
              if tc == 0:
                  hmod_mm_done = {}
              hmod_mm_done[tc] = P.val("mm")

              for oc in range(16):
                  pw = pwa if oc % 2 == 0 else pwb

                  @blk.tensor
                  def _(e, oc=oc, pw=pw, t_hT=t_hT, tc=tc):
                      if oc == 0:
                          P.wait(e, "dve", t_hT)
                      gk = tc * 16 + oc
                      if gk >= 2:
                          P.wait(e, "evac", evac_done[gk - 2])
                      for kc in range(NKC):
                          P.inc("mm", e.matmul(
                              pw, win_bf[:, kc * 2048 + oc * 128:kc * 2048 + (oc + 1) * 128],
                              hT[:, kc * TCP1:(kc + 1) * TCP1],
                              start=(kc == 0), stop=(kc == NKC - 1)), 1)
                  t_win = P.val("mm")
                  if tc == 0 and oc == 0:
                      evac_done = {}

                  gk = tc * 16 + oc
                  if oc < 8:
                      @blk.vector
                      def _(e, oc=oc, pw=pw, t_win=t_win, gk=gk):
                          P.wait(e, "mm", t_win)
                          ins = e.tensor_copy(
                              xs1[:, oc * (TCP1 + 4) + 4:(oc + 1) * (TCP1 + 4)], pw)
                          evac_done[gk] = P.inc("evac", ins, 1)
                  else:
                      @blk.scalar
                      def _(e, oc=oc, pw=pw, t_win=t_win, gk=gk, tc=tc):
                          P.wait(e, "mm", t_win)
                          if oc == 8 and tc > 0:
                              P.wait2(e, sr_spilled_d[tc - 1])
                          ins = e.activation(src1[:, (oc - 8) * TCP1:(oc - 7) * TCP1],
                                             pw, AF.Silu)
                          evac_done[gk] = P.inc("evac", ins, 1)
              t_evac = P.val("evac")
              t_sr = t_evac
              t_xs = t_evac
              hT_free[tc] = P.val("mm")

              @blk.gpsimd
              def _(e, tc=tc, t_sr=t_sr):
                  P.wait(e, "evac", t_sr)
                  P.inc("d16", e.dma_start(
                      out=sr_sp.ap()[:, :, tc * TCP1:(tc + 1) * TCP1]
                          .rearrange("d p t -> p d t"),
                      in_=src1.rearrange("p (d t) -> p d t", d=NDBLK)), 16)
              if tc == 0:
                  sr_spilled_d = {}
              sr_spilled_d[tc] = P.tval("d16")

              for db in range(NDBLK):
                  xsd = xs1[:, db * (TCP1 + 4):(db + 1) * (TCP1 + 4)]
                  xcd = xcc[:, db * TCP1:(db + 1) * TCP1]

                  @blk.vector
                  def _(e, db=db, xsd=xsd, xcd=xcd, t_xs=t_xs, tc=tc):
                      if db == 0:
                          P.wait(e, "evac", t_xs)
                          if tc == 0:
                              P.wait2(e, t_const)
                              P.wait(e, "dve", t_halo0)
                          if tc > 0:
                              P.wait2(e, xc_spilled[tc - 1])
                      P.inc("dve", e.tensor_scalar_mul(
                          xcd, xsd[:, 1:1 + TCP1], convw[:, db * 4:db * 4 + 1]), 1)
                      for j in range(1, 4):
                          P.wait(e, "dve", P.val("dve"))
                          P.inc("dve", e.scalar_tensor_tensor(
                              xcd, xsd[:, 1 + j:1 + j + TCP1],
                              convw[:, db * 4 + j:db * 4 + j + 1],
                              xcd, ALU.mult, ALU.add), 1)
                  t_conv = P.val("dve")
                  if tc == 0 and db == 0:
                      xc_spilled = {}

                  @blk.scalar
                  def _(e, db=db, xcd=xcd, t_conv=t_conv):
                      P.wait(e, "dve", t_conv)
                      P.inc("act", e.activation(xcd, xcd, AF.Silu,
                                                bias=convb[:, db:db + 1]), 1)
              t_xc = P.val("act")

              @blk.vector
              def _(e, t_xc=t_xc):
                  P.wait(e, "act", t_xc)
                  for db in range(NDBLK):
                      xsd = xs1[:, db * (TCP1 + 4):(db + 1) * (TCP1 + 4)]
                      P.inc("dve", e.tensor_copy(xsd[:, 1:4], xsd[:, TCP1 + 1:TCP1 + 4]), 1)
              t_halo = P.val("dve")
              xin_free[tc] = t_halo

              @blk.gpsimd
              def _(e, tc=tc, t_xc=t_xc):
                  P.wait(e, "act", t_xc)
                  P.inc("d17", e.dma_start(
                      out=xc_sp.ap()[:, :, tc * TCP1:(tc + 1) * TCP1]
                          .rearrange("d p t -> p d t"),
                      in_=xcc.rearrange("p (d t) -> p d t", d=NDBLK)), 16)
                  xc_spilled[tc] = P.tval("d17")

              @blk.tensor
              def _(e, t_xc=t_xc, tc=tc):
                  P.wait(e, "act", t_xc)
                  if tc > 0:
                      P.wait(e, "dve", xd_free[tc - 1])
                  for db in range(NDBLK):
                      P.inc("mm", e.matmul(
                          pxd, wx_bf[:, db * 128:(db + 1) * 128],
                          xcc[:, db * TCP1:(db + 1) * TCP1],
                          start=(db == 0), stop=(db == NDBLK - 1)), 1)
              t_xd = P.val("mm")

              @blk.vector
              def _(e, t_xd=t_xd, tc=tc):
                  P.wait(e, "mm", t_xd)
                  if tc > 0:
                      P.wait2(e, xdp_spilled[tc - 1])
                  P.inc("dve", e.tensor_copy(xdp, pxd), 1)
                  xd_free[tc] = P.val("dve")
              t_xdev = P.val("dve")

              @blk.gpsimd
              def _(e, tc=tc, t_xdev=t_xdev):
                  P.wait(e, "dve", t_xdev)
                  P.inc("d_sp_xd", e.dma_start(
                      out=xdbl_part.ap()[:, tc * TCP1:(tc + 1) * TCP1],
                      in_=xdp), 16)
                  xdp_spilled[tc] = P.tval("d_sp_xd")

      _st.close()
      return P.tval("d_sp_xd")

    t_p1sp = _phase1()

    @blk.scalar
    def _(e):
        P.wait2(e, t_p1sp)
        P.inc("sgxc", e.dma_start(
            out=sh_xdbl.ap()[bass.ts(regs["a_rp"], 1)].squeeze(0),
            in_=xdbl_part.ap()), 16)
    t_p1cp = P.tval("sgxc")

    def emit_signal(ready_tv, fence_ap, flag_col, sg):
        @blk.gpsimd
        def _(e):
            P.wait2(e, ready_tv)
            rp = regs["g_rp"]
            P.inc(sg, e.dma_start(out=fence_rd, in_=fence_ap(rp)), 16)
            P.wait(e, sg, P.val(sg))
            P.inc(sg, e.dma_start(out=flag_src,
                                  in_=meta_h.ap()[0:1, flag_col:flag_col + 1]), 16)
            P.wait(e, sg, P.val(sg))
            P.inc(sg, e.dma_start(
                out=sh_flag.ap()[bass.ts(rp, 1)].squeeze(0), in_=flag_src), 16)
            P.wait(e, sg, P.val(sg))
        return P.tval(sg)

    def emit_poll(t_sig, flag_col, done_sem, poll_sem):
        @blk.sync
        def _(e):
            P.wait2(e, t_sig)
            rq = regs["s_rq"]
            with e.register("rflag_" + done_sem) as rflag, \
                 e.register("rcond_" + done_sem) as rcond, \
                 e.register("rexp_" + done_sem) as rexp, \
                 e.register("rtgt_" + done_sem) as rtgt:
                e.reg_mov(rexp, 0)
                e.reg_mov(rflag, -(2 ** 31) + 10)
                e.load(rtgt, meta[0:1, flag_col:flag_col + 1])
                e.reg_alu(rcond, rflag, rtgt, ALU.not_equal)
                with e.While(rcond):
                    e.reg_add(rexp, rexp, 16)
                    e.dma_start(out=flag_rd,
                                in_=sh_flag.ap()[bass.ts(rq, 1)].squeeze(0)) \
                        .then_inc(P.sem(poll_sem)[0], 16)
                    e.wait_ge(P.sem(poll_sem)[0], rexp)
                    e.load(rflag, flag_rd[0:1, 0:1])
                    e.reg_alu(rcond, rflag, rtgt, ALU.not_equal)
            P.inc(done_sem, e.nop(), 1)
        return P.val(done_sem)

    t_e1sig = emit_signal(
        t_p1cp, lambda rp: sh_xdbl.ap()[bass.ts(rp, 1)].squeeze(0)[0:1, L - 4:L], 2,
        "sg1")
    t_e1 = emit_poll(t_e1sig, 2, "e1", "po1")

    _st = contextlib.ExitStack()
    xda_t = _st.enter_context(nc.sbuf_tensor("xda", [128, L], F32))
    xdb_t = _st.enter_context(nc.sbuf_tensor("xdb", [128, L], F32))
    if True:
        xda = xda_t.ap(); xdb = xdb_t.ap()

        @blk.sync
        def _(e):
            P.wait(e, "e1", t_e1)
            P.inc("d18", e.dma_start(out=xda, in_=sh_xdbl.ap()[0]), 16)
            P.inc("d18", e.dma_start(out=xdb, in_=sh_xdbl.ap()[1]), 16)
        t_xdl = P.tval("d18")

        @blk.vector
        def _(e):
            P.wait2(e, t_xdl)
            P.inc("dve", e.tensor_add(xda, xda, xdb), 1)
            P.wait(e, "dve", P.val("dve"))
            P.inc("dve", e.tensor_copy(xdbl_bf, xda), 1)
        t_xdred = P.val("dve")

        @blk.gpsimd
        def _(e):
            P.wait(e, "dve", t_xdred)
            P.inc("d20", e.dma_start(out=bc_sp.ap()[0:N, :], in_=xdbl_bf[64:80, :]), 16)
            P.inc("d20", e.dma_start(out=bc_sp.ap()[N:2 * N, :], in_=xdbl_bf[96:112, :]), 16)
    _st.close()
    t_bcsp = P.tval("d20")

    def _phase2():
      _st = contextlib.ExitStack()
      zsb_t = _st.enter_context(nc.sbuf_tensor("zsb", [128, TC2], F32))
      znz_t = _st.enter_context(nc.sbuf_tensor("znz", [128, TC2], F32))
      zmn_t = _st.enter_context(nc.sbuf_tensor("zmn", [128, TC2], F32))
      zm_t = _st.enter_context(nc.sbuf_tensor("zm", [128, TC2], F32))
      dlt_t = _st.enter_context(nc.sbuf_tensor("dlt", [128, NDBLK * TC2], BF16))
      dut_t = _st.enter_context(nc.sbuf_tensor("dut", [128, NDBLK * TC2], BF16))
      xcr_t = _st.enter_context(nc.sbuf_tensor("xcr", [128, 2 * NDBLK * TC2], BF16))
      srr_t = _st.enter_context(nc.sbuf_tensor("srr", [128, NDBLK * TC2], BF16))
      bbc_t = _st.enter_context(nc.sbuf_tensor("bbc", [128, N * TC2], BF16))
      cbc_t = _st.enter_context(nc.sbuf_tensor("cbc", [128, N * TC2], BF16))
      dAb_t = _st.enter_context(nc.sbuf_tensor("dAb", [128, 2 * TC2], BF16))
      rrb_t = _st.enter_context(nc.sbuf_tensor("rrb", [128, 2 * TC2], BF16))
      hhb_t = _st.enter_context(nc.sbuf_tensor("hhb", [128, 2 * TC2], BF16))
      wwb_t = _st.enter_context(nc.sbuf_tensor("wwb", [128, 2 * TC2], BF16))
      ybf_t = _st.enter_context(nc.sbuf_tensor("ybf", [128, TC2], BF16))
      ysr_t = _st.enter_context(nc.sbuf_tensor("ysr", [128, NDBLK * TC2], BF16))
      atev_t = _st.enter_context(nc.sbuf_tensor("atev", [128, 2 * TC2], F32))
      pz_t = _st.enter_context(nc.psum_tensor("pz", [128, TC2], F32))
      py_t = _st.enter_context(nc.psum_tensor("py", [128, TC2], F32))
      pat_t = _st.enter_context(nc.psum_tensor("pat", [128, TC2], F32))
      pat2_t = _st.enter_context(nc.psum_tensor("pat2", [128, TC2], F32))
      if True:
          zsb = zsb_t.ap(); znz = znz_t.ap(); zmn = zmn_t.ap(); zm = zm_t.ap()
          dlt = dlt_t.ap(); dut = dut_t.ap(); xcr = xcr_t.ap(); srr = srr_t.ap()
          bbc = bbc_t.ap(); cbc = cbc_t.ap()
          dAb = dAb_t.ap(); rrb = rrb_t.ap(); hhb = hhb_t.ap(); wwb = wwb_t.ap()
          ybf = ybf_t.ap(); ysr = ysr_t.ap(); atev = atev_t.ap()
          pz = pz_t.ap(); py = py_t.ap(); pat = pat_t.ap(); pat2 = pat2_t.ap()

          xc_free = {}
          dve_ld_gate = {}
          at_free = {}
          slab_done = {}
          zfree = []
          scan_mm = []

          for tc in range(NTC2):
              xcb = xcr[:, (tc % 2) * NDBLK * TC2:((tc % 2) + 1) * NDBLK * TC2]

              @blk.sync
              def _(e, tc=tc, xcb=xcb):
                  if tc == 0:
                      P.wait2(e, t_bcsp)
                  if tc >= 2:
                      P.wait(e, "dve", xc_free[tc - 2])
                  P.inc("d22", e.dma_start(
                      out=xcb.rearrange("p (d t) -> p d t", d=NDBLK),
                      in_=xc_sp.ap()[:, :, tc * TC2:(tc + 1) * TC2]
                      .rearrange("d p t -> p d t")), 16)
                  if tc >= 1:
                      P.wait(e, "dve", dve_ld_gate[tc - 1])
                  P.inc("d22", e.dma_start(
                      out=srr.rearrange("p (d t) -> p d t", d=NDBLK),
                      in_=sr_sp.ap()[:, :, tc * TC2:(tc + 1) * TC2]
                      .rearrange("d p t -> p d t")), 16)
                  P.inc("d22", e.dma_start(
                      out=bbc.rearrange("p (n t) -> p n t", n=N),
                      in_=bc_sp.ap()[0:N, tc * TC2:(tc + 1) * TC2]
                      .unsqueeze(0).partition_broadcast(128)), 16)
                  P.inc("d22", e.dma_start(
                      out=cbc.rearrange("p (n t) -> p n t", n=N),
                      in_=bc_sp.ap()[N:2 * N, tc * TC2:(tc + 1) * TC2]
                      .unsqueeze(0).partition_broadcast(128)), 16)
              t_ld = P.tval("d22")

              for db in range(NDBLK):
                  @blk.tensor
                  def _(e, db=db, tc=tc):
                      if db == 0 and tc == 0:
                          P.wait(e, "dve", t_xdred)
                      if zfree:
                          P.wait(e, "dve", zfree[-1])
                      P.inc("mm", e.matmul(
                          pz, wdt_bf[:, db * 128:(db + 1) * 128],
                          xdbl_bf[0:64, tc * TC2:(tc + 1) * TC2],
                          start=True, stop=True), 1)
                  t_z = P.val("mm")

                  @blk.vector
                  def _(e, db=db, t_z=t_z, tc=tc):
                      P.wait(e, "mm", t_z)
                      P.inc("dve", e.tensor_scalar_add(zsb, pz, bdt[:, db:db + 1]), 1)
                      zfree.append(P.val("dve"))
                      P.wait(e, "dve", P.val("dve"))
                      P.inc("dve", e.tensor_scalar_mul(znz, zsb, -1.0), 1)
                      P.wait(e, "dve", P.val("dve"))
                      P.inc("dve", e.tensor_tensor(zmn, zsb, znz, ALU.min), 1)
                  t_zmn = P.val("dve")

                  @blk.scalar
                  def _(e, t_zmn=t_zmn):
                      P.wait(e, "dve", t_zmn)
                      P.inc("act", e.activation(zmn, zmn, AF.Exp), 1)
                      P.inc("act", e.activation(zmn, zmn, AF.Ln, bias=1.0), 1)
                  t_lnp = P.val("act")

                  @blk.vector
                  def _(e, db=db, t_lnp=t_lnp, t_ld=t_ld, xcb=xcb, tc=tc):
                      P.wait(e, "act", t_lnp)
                      if db == 0 and tc > 0:
                          P.wait(e, "act", last_dA_act[tc - 1])
                      P.inc("dve", e.tensor_scalar_max(zm, zsb, 0.0), 1)
                      P.wait(e, "dve", P.val("dve"))
                      P.inc("dve", e.tensor_add(dlt[:, db * TC2:(db + 1) * TC2], zm, zmn), 1)
                      if db == 0:
                          P.wait2(e, t_ld)
                      P.wait(e, "dve", P.val("dve"))
                      P.inc("dve", e.tensor_mul(
                          dut[:, db * TC2:(db + 1) * TC2],
                          dlt[:, db * TC2:(db + 1) * TC2],
                          xcb[:, db * TC2:(db + 1) * TC2]), 1)
              t_dlt = P.val("dve")
              if tc == 0:
                  pass

              for db in range(NDBLK):
                  for n in range(N):
                      it = db * N + n
                      sl = it % 2
                      dAs = dAb[:, sl * TC2:(sl + 1) * TC2]
                      rrs = rrb[:, sl * TC2:(sl + 1) * TC2]
                      hhs = hhb[:, sl * TC2:(sl + 1) * TC2]
                      wws = wwb[:, sl * TC2:(sl + 1) * TC2]
                      col = it

                      @blk.scalar
                      def _(e, db=db, n=n, dAs=dAs, t_dlt=t_dlt, it=it, tc=tc):
                          if it == 0:
                              P.wait(e, "dve", t_dlt)
                          if it == 0 and tc == 0:
                              P.wait(e, "dve", t_aneg)
                          gi = tc * 128 + it
                          if gi >= 2:
                              P.wait(e, "dve", dA_free[gi - 2])
                          P.inc("act", e.activation(
                              dAs, dlt[:, db * TC2:(db + 1) * TC2], AF.Exp,
                              scale=a_sb[:, it:it + 1]), 1)
                      t_dA = P.val("act")
                      if tc == 0 and it == 0:
                          dA_free = {}

                      @blk.vector
                      def _(e, db=db, n=n, dAs=dAs, rrs=rrs, hhs=hhs, wws=wws,
                            col=col, t_dA=t_dA, tc=tc, it=it):
                          gi = tc * 128 + it
                          if gi >= 2 and scan_mm:
                              P.wait(e, "mm", scan_mm[gi - 2])
                          P.inc("dve", e.tensor_mul(
                              rrs, dut[:, db * TC2:(db + 1) * TC2],
                              bbc[:, n * TC2:(n + 1) * TC2]), 1)
                          P.wait(e, "act", t_dA)
                          init = 0.0 if tc == 0 else state[:, col:col + 1]
                          P.inc("dve", e.tensor_tensor_scan(
                              hhs, dAs, rrs, init, ALU.mult, ALU.add), 1)
                          dA_free[gi] = P.val("dve")
                          P.wait(e, "dve", P.val("dve"))
                          P.inc("dve", e.tensor_copy(state[:, col:col + 1],
                                                     hhs[:, TC2 - 1:TC2]), 1)
                          P.inc("dve", e.tensor_mul(
                              wws, hhs, cbc[:, n * TC2:(n + 1) * TC2]), 1)
                      t_w = P.val("dve")

                      @blk.tensor
                      def _(e, n=n, wws=wws, t_w=t_w, db=db):
                          P.wait(e, "dve", t_w)
                          if n == 0:
                              P.wait(e, "act", P.val("act"))
                          P.inc("mm", e.matmul(py, ident_bf, wws,
                                               start=(n == 0), stop=(n == N - 1)), 1)
                          scan_mm.append(P.val("mm"))
                  t_y = P.val("mm")

                  @blk.scalar
                  def _(e, t_y=t_y):
                      P.wait(e, "mm", t_y)
                      P.inc("act", e.activation(ybf, py, AF.Copy), 1)
                  t_ybf = P.val("act")

                  @blk.vector
                  def _(e, db=db, t_ybf=t_ybf, tc=tc):
                      P.wait(e, "act", t_ybf)
                      if tc > 0 and db == 0:
                          P.wait(e, "mm", at_done[tc - 1])
                      P.inc("dve", e.tensor_mul(
                          ysr[:, db * TC2:(db + 1) * TC2], ybf,
                          srr[:, db * TC2:(db + 1) * TC2]), 1)
              t_ysr = P.val("dve")
              xc_free[tc] = t_ysr
              dve_ld_gate[tc] = t_ysr
              if tc == 0:
                  last_dA_act = {}
              last_dA_act[tc] = P.val("act")
              if tc == 0:
                  at_done = {}

              for hc in range(NKC):
                  pp = pat if hc % 2 == 0 else pat2
                  asl = hc % 2

                  @blk.tensor
                  def _(e, hc=hc, pp=pp, t_ysr=t_ysr, tc=tc):
                      if hc == 0:
                          P.wait(e, "dve", t_ysr)
                      if tc * NKC + hc >= 2:
                          P.wait2(e, at_free[tc * NKC + hc - 2])
                      for db in range(NDBLK):
                          P.inc("mm", e.matmul(
                              pp, wout_bf[:, db * H + hc * 128:db * H + (hc + 1) * 128],
                              ysr[:, db * TC2:(db + 1) * TC2],
                              start=(db == 0), stop=(db == NDBLK - 1)), 1)
                  t_at = P.val("mm")
                  if tc == NTC2 - 1 and hc == NKC - 1:
                      at_done[tc] = t_at
                  if hc == NKC - 1:
                      at_done[tc] = t_at

                  @blk.vector
                  def _(e, pp=pp, asl=asl, t_at=t_at):
                      P.wait(e, "mm", t_at)
                      P.inc("dve", e.tensor_copy(atev[:, asl * TC2:(asl + 1) * TC2], pp), 1)
                  t_atev = P.val("dve")

                  @blk.gpsimd
                  def _(e, hc=hc, tc=tc, asl=asl, t_atev=t_atev):
                      P.wait(e, "dve", t_atev)
                      P.inc("spat%d" % asl, e.dma_start(
                          out=attn_stage.ap()[hc, :, tc * TC2:(tc + 1) * TC2],
                          in_=atev[:, asl * TC2:(asl + 1) * TC2]), 16)
                      at_free[tc * NKC + hc] = P.tval("spat%d" % asl)

          @blk.scalar
          def _(e):
              P.wait2(e, at_free[(NTC2 - 1) * NKC + NKC - 2])
              P.wait2(e, at_free[(NTC2 - 1) * NKC + NKC - 1])
              P.inc("slab", e.dma_start(
                  out=sh_attn.ap()[bass.ts(regs["a_rp"], 1)].squeeze(0),
                  in_=attn_stage.ap()), 16)
              slab_done[0] = P.tval("slab")

      _st.close()
      return slab_done[0]

    t_p2sp = _phase2()

    t_e2sig = emit_signal(
        t_p2sp,
        lambda rp: sh_attn.ap()[bass.ts(rp, 1)].squeeze(0)[NKC - 1, 0:1, L - 4:L], 3,
        "sg2")
    t_e2 = emit_poll(t_e2sig, 3, "e2", "po2")

    @blk.sync
    def _(e):
        P.wait(e, "e2", t_e2)
        rt = regs["s_rt"]
        for s in range(2):
            P.inc("d26", e.dma_start(
                out=attn_in.ap()[s],
                in_=sh_attn.ap()[s][:, :, bass.DynSlice(rt, THALF)]), 16)
    t_gather = P.tval("d26")

    ga_rep = AL("ga_rep", [128, H], F32)
    sb_rep = AL("sb_rep", [128, H], F32)
    cb1_rep = AL("cb1_rep", [128, H], F32)
    gb_rep = AL("gb_rep", [128, H], F32)

    @blk.sync
    def _(e):
        P.wait(e, "e2", t_e2)
        P.wait2(e, t_gather)
        for i, dst in enumerate([ga_rep, sb_rep, cb1_rep, gb_rep]):
            seg = [2, 3, 4, 5][i]
            P.inc("d27", e.dma_start(
                out=dst, in_=mod_sp.ap()[seg * H:(seg + 1) * H]
                .unsqueeze(0).partition_broadcast(128)), 16)
    t_p3rep = P.tval("d27")

    def _phase3a():
      _st = contextlib.ExitStack()
      a0t_t = _st.enter_context(nc.sbuf_tensor("a0t", [128, 2 * TC], F32))
      a1t_t = _st.enter_context(nc.sbuf_tensor("a1t", [128, 2 * TC], F32))
      x3_t = _st.enter_context(nc.sbuf_tensor("x3", [128, 2 * H], F32))
      atT_t = _st.enter_context(nc.sbuf_tensor("at_T", [128, H], F32))
      x1c_t = _st.enter_context(nc.sbuf_tensor("x1c", [128, H], F32))
      lnsc3_t = _st.enter_context(nc.sbuf_tensor("lnsc3", [128, 16], F32))
      sq3_t = _st.enter_context(nc.sbuf_tensor("sq3", [128, H], F32))
      h2tm_t = _st.enter_context(nc.sbuf_tensor("h2tm", [128, H], BF16))
      h2T_t = _st.enter_context(nc.sbuf_tensor("h2T", [128, NKC * 128], BF16))
      ptp3_t = _st.enter_context(nc.psum_tensor("ptp3", [128, 128], F32))
      ptp3b_t = _st.enter_context(nc.psum_tensor("ptp3b", [128, 128], BF16))
      psum3_t = _st.enter_context(nc.psum_tensor("psum3", [1, H], F32))
      if True:
          a0t = a0t_t.ap(); a1t = a1t_t.ap(); x3 = x3_t.ap(); atT = atT_t.ap()
          x1c = x1c_t.ap(); lnsc3 = lnsc3_t.ap(); sq3 = sq3_t.ap()
          h2tm = h2tm_t.ap(); h2T = h2T_t.ap()
          ptp3 = ptp3_t.ap(); ptp3b = ptp3b_t.ap(); psum3 = psum3_t.ap()

          a_ld_free = {}
          x1_spilled = {}
          h2_spilled = {}
          for ti in range(THALF // 128):
              sl = ti % 2

              @blk.sync
              def _(e, ti=ti, sl=sl):
                  if ti == 0:
                      P.wait2(e, t_p3rep)
                  if ti >= 2:
                      P.wait(e, "dve", a_ld_free[ti - 2])
                  P.inc("d28", e.dma_start(
                      out=a0t[:, sl * TC:sl * TC + TC].rearrange("p (k t) -> p k t", k=NKC),
                      in_=attn_in.ap()[0].transpose([1, 0, 2])
                      [:, :, ti * 128:(ti + 1) * 128]), 16)
                  P.inc("d28", e.dma_start(
                      out=a1t[:, sl * TC:sl * TC + TC].rearrange("p (k t) -> p k t", k=NKC),
                      in_=attn_in.ap()[1].transpose([1, 0, 2])
                      [:, :, ti * 128:(ti + 1) * 128]), 16)
                  P.inc("d28", e.dma_start(
                      out=x3[:, sl * H:(sl + 1) * H],
                      in_=xo_h.ap()[ti * 128:(ti + 1) * 128, :]), 16)
              t_l3 = P.tval("d28")

              @blk.vector
              def _(e, sl=sl, t_l3=t_l3):
                  P.wait2(e, t_l3)
                  P.inc("dve", e.tensor_add(a0t[:, sl * TC:sl * TC + TC],
                                            a0t[:, sl * TC:sl * TC + TC],
                                            a1t[:, sl * TC:sl * TC + TC]), 1)
              t_aadd = P.val("dve")

              for kc in range(NKC):
                  @blk.tensor
                  def _(e, kc=kc, sl=sl, t_aadd=t_aadd):
                      if kc == 0:
                          P.wait(e, "dve", t_aadd)
                      else:
                          P.wait(e, "dve", P.val("dve"))
                      P.inc("mm", e.transpose(
                          ptp3, a0t[:, sl * TC + kc * 128:sl * TC + (kc + 1) * 128],
                          ident_f32), 1)
                  t_tp3 = P.val("mm")

                  @blk.vector
                  def _(e, kc=kc, t_tp3=t_tp3):
                      P.wait(e, "mm", t_tp3)
                      P.inc("dve", e.tensor_copy(atT[:, kc * 128:(kc + 1) * 128], ptp3), 1)
              t_atT = P.val("dve")

              @blk.vector
              def _(e, sl=sl, t_atT=t_atT, ti=ti):
                  if ti > 0:
                      P.wait2(e, x1_spilled[ti - 1])
                  P.wait(e, "dve", P.val("dve"))
                  P.inc("dve", e.tensor_mul(atT, atT, ga_rep), 1)
                  P.wait(e, "dve", P.val("dve"))
                  P.inc("dve", e.tensor_add(x1c, x3[:, sl * H:(sl + 1) * H], atT), 1)
              t_x1 = P.val("dve")
              a_ld_free[ti] = t_x1

              @blk.gpsimd
              def _(e, ti=ti, t_x1=t_x1):
                  P.wait(e, "dve", t_x1)
                  P.inc("d31", e.dma_start(
                      out=x1_sp.ap()[ti * 128:(ti + 1) * 128, :], in_=x1c), 16)
                  x1_spilled[ti] = P.tval("d31")
              if ti == 0:
                  pass

              @blk.vector
              def _(e):
                  P.wait(e, "dve", P.val("dve"))
                  P.inc("dve", e.reduce_sum(lnsc3[:, 0:1], x1c, axis=AX.X), 1)
              t_rs3 = P.val("dve")

              @blk.scalar
              def _(e, t_rs3=t_rs3):
                  P.wait(e, "dve", t_rs3)
                  P.inc("act", e.activation(sq3, x1c, AF.Square,
                                            accum_out=lnsc3[:, 1:2]), 1)
              t_sq3 = P.val("act")

              @blk.vector
              def _(e, t_sq3=t_sq3):
                  P.wait(e, "act", t_sq3)
                  P.inc("dve", e.tensor_scalar_mul(lnsc3[:, 2:3], lnsc3[:, 0:1], 1.0 / H), 1)
                  P.inc("dve", e.tensor_scalar_mul(lnsc3[:, 3:4], lnsc3[:, 1:2], 1.0 / H), 1)
                  P.wait(e, "dve", P.val("dve"))
                  P.inc("dve", e.tensor_mul(lnsc3[:, 4:5], lnsc3[:, 2:3], lnsc3[:, 2:3]), 1)
                  P.wait(e, "dve", P.val("dve"))
                  P.inc("dve", e.tensor_sub(lnsc3[:, 5:6], lnsc3[:, 3:4], lnsc3[:, 4:5]), 1)
              t_var3 = P.val("dve")

              @blk.scalar
              def _(e, t_var3=t_var3):
                  P.wait(e, "dve", t_var3)
                  P.inc("act", e.activation(lnsc3[:, 6:7], lnsc3[:, 5:6], AF.Ln, bias=EPS), 1)
                  P.inc("act", e.activation(lnsc3[:, 7:8], lnsc3[:, 6:7], AF.Exp, scale=-0.5), 1)
              t_rstd3 = P.val("act")

              @blk.vector
              def _(e, t_rstd3=t_rstd3, ti=ti):
                  P.wait(e, "act", t_rstd3)
                  if ti > 0:
                      P.wait(e, "mm", h2_mm_done[ti - 1])
                  P.inc("dve", e.tensor_mul(lnsc3[:, 8:9], lnsc3[:, 2:3], lnsc3[:, 7:8]), 1)
                  P.wait(e, "dve", P.val("dve"))
                  P.inc("dve", e.tensor_scalar(sq3, x1c, lnsc3[:, 7:8], lnsc3[:, 8:9],
                                               ALU.mult, ALU.subtract), 1)
                  P.wait(e, "dve", P.val("dve"))
                  P.inc("dve", e.tensor_mul(sq3, sq3, cb1_rep), 1)
                  P.wait(e, "dve", P.val("dve"))
                  P.inc("dve", e.tensor_add(h2tm, sq3, sb_rep), 1)
              t_h2 = P.val("dve")

              @blk.tensor
              def _(e, ti=ti, t_h2=t_h2):
                  P.wait(e, "dve", t_h2)
                  for hv in range(2):
                      P.inc("mm", e.matmul(
                          psum3[:, hv * 512:(hv + 1) * 512], ones_bf,
                          h2tm[:, hv * 512:(hv + 1) * 512],
                          start=(ti == 0), stop=(ti == THALF // 128 - 1)), 1)
              t_hsum = P.val("mm")

              for kc in range(NKC):
                  @blk.tensor
                  def _(e, kc=kc, t_h2=t_h2):
                      P.wait(e, "dve", P.val("dve"))
                      P.inc("mm", e.transpose(
                          ptp3b, h2tm[:, kc * 128:(kc + 1) * 128], ident_bf), 1)
                  t_tph = P.val("mm")

                  @blk.vector
                  def _(e, kc=kc, t_tph=t_tph, ti=ti):
                      P.wait(e, "mm", t_tph)
                      if kc == 0 and ti > 0:
                          P.wait2(e, h2_spilled[ti - 1])
                      P.inc("dve", e.tensor_copy(h2T[:, kc * 128:(kc + 1) * 128], ptp3b), 1)
              t_h2T = P.val("dve")
              if ti == 0:
                  h2_mm_done = {}
              h2_mm_done[ti] = P.val("mm")

              @blk.gpsimd
              def _(e, ti=ti, t_h2T=t_h2T):
                  P.wait(e, "dve", t_h2T)
                  P.inc("d32", e.dma_start(
                      out=h2_sp.ap()[:, :, ti * 128:(ti + 1) * 128]
                      .rearrange("k p t -> p k t"),
                      in_=h2T.rearrange("p (k t) -> p k t", k=NKC)), 16)
                  h2_spilled[ti] = P.tval("d32")
          t_p3asp = P.tval("d32")
          del h2_mm_done

          hsum_t = _st.enter_context(nc.sbuf_tensor("hsum", [1, H], F32))
          if True:
              hsum = hsum_t.ap()

              @blk.vector
              def _(e):
                  P.wait(e, "mm", t_hsum)
                  P.inc("dve", e.tensor_copy(hsum, psum3), 1)
              t_hsev = P.val("dve")

              @blk.gpsimd
              def _(e):
                  P.wait(e, "dve", t_hsev)
                  rp = regs["g_rp"]
                  P.inc("d_sp_sum", e.dma_start(
                      out=sh_sum.ap()[bass.ts(rp, 1)].squeeze(0), in_=hsum), 16)

      _st.close()
      xfer["x1_last"] = x1_spilled[THALF // 128 - 1]
      xfer["h2_last"] = h2_spilled[THALF // 128 - 1]
      return P.tval("d_sp_sum")

    t_e3w = _phase3a()

    t_e3sig = emit_signal(
        t_e3w, lambda rp: sh_sum.ap()[bass.ts(rp, 1)].squeeze(0)[0:1, H - 4:H], 4,
        "sg3")
    t_e3 = emit_poll(t_e3sig, 4, "e3", "po3")

    cn1_bf = AL("cn1_bf", [128, NKC * H], BF16)
    cn2_bf = AL("cn2_bf", [128, NKC * 512], BF16)
    cn3_bf = AL("cn3_bf", [128, 4 * 256], BF16)
    cn4_bf = AL("cn4_bf", [128, 2], BF16)
    gc1_bf = AL("gc1_bf", [128, NKC * 512], BF16)
    gc2_bf = AL("gc2_bf", [128, 4 * 256], BF16)
    gc3_bf = AL("gc3_bf", [128, 2], BF16)
    cnb1 = AL("cnb1_s", [128, 8], F32)
    cnb2 = AL("cnb2_s", [128, 4], F32)
    cnb3 = AL("cnb3_s", [128, 2], F32)
    cnb4 = AL("cnb4_s", [1, 1], F32)
    gcb1 = AL("gcb1_s", [128, 4], F32)
    gcb2 = AL("gcb2_s", [128, 2], F32)
    gcb3 = AL("gcb3_s", [1, 1], F32)
    arv = AL("arv_s", [1, 1], F32)

    wstg_t = _st.enter_context(nc.sbuf_tensor("wstg", [128, NKC * H], F32))
    if True:
        wstg = wstg_t.ap()
        for (dst, src, cols, kcs) in [
            (cn1_bf, cnw1_h, H, NKC), (cn2_bf, cnw2_h, 512, NKC),
            (cn3_bf, cnw3_h, 256, 4), (cn4_bf, cnw4_h, 2, 1),
            (gc1_bf, gcw1_h, 512, NKC), (gc2_bf, gcw2_h, 256, 4),
            (gc3_bf, gcw3_h, 2, 1),
        ]:
            @blk.sync
            def _(e, dst=dst, src=src, cols=cols, kcs=kcs):
                if dst is cn1_bf:
                    P.wait(e, "e3", t_e3)
                else:
                    P.wait(e, "dve", P.val("dve"))
                P.inc("d33", e.dma_start(
                    out=wstg[:, 0:kcs * cols].rearrange("p (kc o) -> p kc o", kc=kcs),
                    in_=src.ap().rearrange("(kc p) o -> p kc o", p=128)), 16)
            t_ws = P.tval("d33")

            @blk.vector
            def _(e, dst=dst, cols=cols, kcs=kcs, t_ws=t_ws):
                P.wait2(e, t_ws)
                P.inc("dve", e.tensor_copy(dst[:, 0:kcs * cols], wstg[:, 0:kcs * cols]), 1)
        t_ffnw = P.val("dve")
        _st.close()

    @blk.sync
    def _(e):
        for dst, src in [(cnb1, cnb1_h), (cnb2, cnb2_h), (cnb3, cnb3_h),
                         (cnb4, cnb4_h), (gcb1, gcb1_h), (gcb2, gcb2_h),
                         (gcb3, gcb3_h), (arv, ar_h)]:
            P.inc("d34", e.dma_start(out=dst, in_=src.ap()), 16)
    t_bload = P.tval("d34")

    gf_sb = AL("gf_sb", [1, 4], F32)
    _st = contextlib.ExitStack()
    msum_t = _st.enter_context(nc.sbuf_tensor("msum", [1, H], F32))
    mzero_t = _st.enter_context(nc.sbuf_tensor("mzero", [128, H], F32))
    gcfm_t = _st.enter_context(nc.sbuf_tensor("gcfm", [128, NKC], BF16))
    gt1_t = _st.enter_context(nc.sbuf_tensor("gt1", [128, 4], F32))
    gt1b_t = _st.enter_context(nc.sbuf_tensor("gt1b", [128, 4], BF16))
    gt2_t = _st.enter_context(nc.sbuf_tensor("gt2", [128, 2], F32))
    gt2b_t = _st.enter_context(nc.sbuf_tensor("gt2b", [128, 2], BF16))
    pg_t = _st.enter_context(nc.psum_tensor("pg", [128, 4], F32))
    pg1_t = _st.enter_context(nc.psum_tensor("pg1", [1, 1], F32))
    pgt_t = _st.enter_context(nc.psum_tensor("pgt", [128, 128], F32))
    if True:
        msum = msum_t.ap(); mzero = mzero_t.ap(); gcfm = gcfm_t.ap()
        gt1 = gt1_t.ap(); gt1b = gt1b_t.ap(); gt2 = gt2_t.ap(); gt2b = gt2b_t.ap()
        pg = pg_t.ap(); pg1 = pg1_t.ap(); pgt = pgt_t.ap()

        @blk.sync
        def _(e):
            P.wait(e, "e3", t_e3)
            P.inc("d35", e.dma_start(out=msum, in_=sh_sum.ap()[0]), 16)
            P.inc("d35", e.dma_start(out=mzero[0:1, :], in_=sh_sum.ap()[1]), 16)
        t_msl = P.tval("d35")

        @blk.vector
        def _(e):
            P.wait2(e, t_msl)
            P.inc("dve", e.tensor_add(msum, msum, mzero[0:1, :]), 1)
            P.wait(e, "dve", P.val("dve"))
            P.inc("dve", e.tensor_scalar_mul(msum, msum, 1.0 / L), 1)
            P.inc("dve", e.memset(mzero, 0.0), 1)
            P.wait(e, "dve", P.val("dve"))
            P.inc("dve", e.tensor_copy(mzero[0:1, :], msum), 1)
        t_gc0 = P.val("dve")

        for kc in range(NKC):
            @blk.tensor
            def _(e, kc=kc):
                if kc == 0:
                    P.wait(e, "dve", t_gc0)
                else:
                    P.wait(e, "dve", P.val("dve"))
                P.inc("mm", e.transpose(pgt, mzero[:, kc * 128:(kc + 1) * 128],
                                        ident_f32), 1)
            t_gt = P.val("mm")

            @blk.vector
            def _(e, kc=kc, t_gt=t_gt):
                P.wait(e, "mm", t_gt)
                P.inc("dve", e.tensor_copy(gcfm[:, kc:kc + 1], pgt[:, 0:1]), 1)
        t_gcfm = P.val("dve")

        @blk.tensor
        def _(e):
            P.wait(e, "dve", t_gcfm)
            P.wait(e, "dve", t_ffnw)
            for oc in range(4):
                for kc in range(NKC):
                    P.inc("mm", e.matmul(
                        pg[:, oc:oc + 1],
                        gc1_bf[:, kc * 512 + oc * 128:kc * 512 + (oc + 1) * 128],
                        gcfm[:, kc:kc + 1],
                        start=(kc == 0), stop=(kc == NKC - 1)), 1)
        t_g1 = P.val("mm")

        @blk.scalar
        def _(e):
            P.wait(e, "mm", t_g1)
            P.wait2(e, t_bload)
            for oc in range(4):
                P.inc("act", e.activation(gt1[:, oc:oc + 1], pg[:, oc:oc + 1],
                                          AF.Gelu, bias=gcb1[:, oc:oc + 1]), 1)
        t_gt1 = P.val("act")

        @blk.vector
        def _(e):
            P.wait(e, "act", t_gt1)
            P.inc("dve", e.tensor_copy(gt1b, gt1), 1)
        t_gt1b = P.val("dve")

        @blk.tensor
        def _(e):
            P.wait(e, "dve", t_gt1b)
            for oc in range(2):
                for kc in range(4):
                    P.inc("mm", e.matmul(
                        pg[:, oc:oc + 1],
                        gc2_bf[:, kc * 256 + oc * 128:kc * 256 + (oc + 1) * 128],
                        gt1b[:, kc:kc + 1],
                        start=(kc == 0), stop=(kc == 3)), 1)
        t_g2m = P.val("mm")

        @blk.scalar
        def _(e):
            P.wait(e, "mm", t_g2m)
            for oc in range(2):
                P.inc("act", e.activation(gt2[:, oc:oc + 1], pg[:, oc:oc + 1],
                                          AF.Gelu, bias=gcb2[:, oc:oc + 1]), 1)
        t_gt2 = P.val("act")

        @blk.vector
        def _(e):
            P.wait(e, "act", t_gt2)
            P.inc("dve", e.tensor_copy(gt2b, gt2), 1)
        t_gt2b = P.val("dve")

        @blk.tensor
        def _(e):
            P.wait(e, "dve", t_gt2b)
            for kc in range(2):
                P.inc("mm", e.matmul(pg1, gc3_bf[:, kc:kc + 1],
                                     gt2b[:, kc:kc + 1],
                                     start=(kc == 0), stop=(kc == 1)), 1)
        t_g3 = P.val("mm")

        @blk.scalar
        def _(e):
            P.wait(e, "mm", t_g3)
            P.inc("act", e.activation(gf_sb[0:1, 0:1], pg1, AF.Sigmoid,
                                      bias=gcb3), 1)
        t_gf = P.val("act")

        @blk.vector
        def _(e):
            P.wait(e, "act", t_gf)
            P.inc("dve", e.tensor_scalar_mul(gf_sb[0:1, 1:2], arv, 0.6), 1)
            P.inc("dve", e.tensor_mul(gf_sb[0:1, 2:3], arv, gf_sb[0:1, 0:1]), 1)
            P.wait(e, "dve", P.val("dve"))
            P.inc("dve", e.tensor_scalar_mul(gf_sb[0:1, 2:3], gf_sb[0:1, 2:3], 1.4), 1)
            P.wait(e, "dve", P.val("dve"))
            P.inc("dve", e.tensor_sub(gf_sb[0:1, 2:3], gf_sb[0:1, 2:3], arv), 1)
            P.wait(e, "dve", P.val("dve"))
            P.inc("dve", e.tensor_scalar_add(gf_sb[0:1, 2:3], gf_sb[0:1, 2:3], 0.1), 1)
        t_k = P.val("dve")
        _st.close()

    TW_SRC = {0: 4, 1: 5, 2: 6, 3: 7, 4: 0, 5: 1, 6: 2, 7: 3}
    TW_SGN = {0: +1, 1: +1, 2: -1, 3: -1, 4: -1, 5: -1, 6: +1, 7: +1}

    def _phase3b():
      _st = contextlib.ExitStack()
      oA_t = _st.enter_context(nc.sbuf_tensor("oA", [128, NKC * TC], BF16))
      oB_t = _st.enter_context(nc.sbuf_tensor("oB", [128, NKC * TC], BF16))
      u1_t = _st.enter_context(nc.sbuf_tensor("u1", [128, NKC * TC], BF16))
      u2_t = _st.enter_context(nc.sbuf_tensor("u2", [128, 4 * TC], BF16))
      u3_t = _st.enter_context(nc.sbuf_tensor("u3", [128, 2 * TC], BF16))
      tf_t = _st.enter_context(nc.sbuf_tensor("tf", [1, TC], F32))
      coup_t = _st.enter_context(nc.sbuf_tensor("coup", [1, TC], BF16))
      cpr_t = _st.enter_context(nc.sbuf_tensor("cpr", [128, 2 * TC], BF16))
      ffT_t = _st.enter_context(nc.sbuf_tensor("ffT", [128, H], F32))
      x1r_t = _st.enter_context(nc.sbuf_tensor("x1r", [128, 2 * H], F32))
      x2c_t = _st.enter_context(nc.sbuf_tensor("x2c", [128, H], F32))
      pf1_t = _st.enter_context(nc.psum_tensor("pf1", [128, TC], F32))
      pf2_t = _st.enter_context(nc.psum_tensor("pf2", [128, TC], F32))
      pf4_t = _st.enter_context(nc.psum_tensor("pf4", [1, TC], F32))
      pcr_t = _st.enter_context(nc.psum_tensor("pcr", [128, TC], F32))

      if True:
          oA = oA_t.ap(); oB = oB_t.ap(); u1 = u1_t.ap(); u2 = u2_t.ap()
          u3 = u3_t.ap(); tf = tf_t.ap(); coup = coup_t.ap(); cpr = cpr_t.ap()
          ffT = ffT_t.ap(); x1r = x1r_t.ap(); x2c = x2c_t.ap()
          pf1 = pf1_t.ap(); pf2 = pf2_t.ap(); pf4 = pf4_t.ap()
          pcr = pcr_t.ap(); ptq = pcr[:, 0:128]

          for tc3 in range(NTC3):
              @blk.sync
              def _(e, tc3=tc3):
                  P.wait(e, "dve", t_k)
                  if tc3 == 0:
                      P.wait2(e, xfer["h2_last"])
                  if tc3 > 0:
                      P.wait2(e, out_written[tc3 - 1])
                  P.inc("d37", e.dma_start(
                      out=oA.rearrange("p (k t) -> p k t", k=NKC),
                      in_=h2_sp.ap()[:, :, tc3 * TC:(tc3 + 1) * TC]
                      .rearrange("k p t -> p k t")), 16)
              t_h2l = P.tval("d37")
              if tc3 == 0:
                  out_written = {}

              cur, nxt = oA, oB
              gate = t_h2l
              for cyc in range(3):
                  for oc in range(8):
                      pf = pf1 if oc % 2 == 0 else pf2

                      @blk.tensor
                      def _(e, oc=oc, pf=pf, cur=cur, cyc=cyc, tc3=tc3,
                            gate=gate):
                          if oc == 0:
                              P.wait2(e, gate)
                              if tc3 == 0 and cyc == 0:
                                  P.wait(e, "dve", t_ffnw)
                          if oc >= 2:
                              P.wait(e, "act", P.val("act"))
                          for hv in range(2):
                              for kc in range(NKC):
                                  P.inc("mm", e.matmul(
                                      pf[:, hv * 512:(hv + 1) * 512],
                                      cn1_bf[:, kc * H + oc * 128:kc * H + (oc + 1) * 128],
                                      cur[:, kc * TC + hv * 512:kc * TC + (hv + 1) * 512],
                                      start=(kc == 0), stop=(kc == NKC - 1)), 1)
                      t_f1 = P.val("mm")

                      @blk.scalar
                      def _(e, oc=oc, pf=pf, t_f1=t_f1):
                          P.wait(e, "mm", t_f1)
                          P.inc("act", e.activation(u1[:, oc * TC:(oc + 1) * TC], pf,
                                                    AF.Gelu, bias=cnb1[:, oc:oc + 1]), 1)
                  t_u1 = P.val("act")

                  for oc in range(4):
                      pf = pf1 if oc % 2 == 0 else pf2

                      @blk.tensor
                      def _(e, oc=oc, pf=pf, t_u1=t_u1):
                          if oc == 0:
                              P.wait(e, "act", t_u1)
                          else:
                              P.wait(e, "act", P.val("act"))
                          for hv in range(2):
                              for kc in range(NKC):
                                  P.inc("mm", e.matmul(
                                      pf[:, hv * 512:(hv + 1) * 512],
                                      cn2_bf[:, kc * 512 + oc * 128:kc * 512 + (oc + 1) * 128],
                                      u1[:, kc * TC + hv * 512:kc * TC + (hv + 1) * 512],
                                      start=(kc == 0), stop=(kc == NKC - 1)), 1)
                      t_f2 = P.val("mm")

                      @blk.scalar
                      def _(e, oc=oc, pf=pf, t_f2=t_f2):
                          P.wait(e, "mm", t_f2)
                          P.inc("act", e.activation(u2[:, oc * TC:(oc + 1) * TC], pf,
                                                    AF.Gelu, bias=cnb2[:, oc:oc + 1]), 1)
                  t_u2 = P.val("act")

                  for oc in range(2):
                      pf = pf1 if oc % 2 == 0 else pf2

                      @blk.tensor
                      def _(e, oc=oc, pf=pf, t_u2=t_u2):
                          if oc == 0:
                              P.wait(e, "act", t_u2)
                          else:
                              P.wait(e, "act", P.val("act"))
                          for hv in range(2):
                              for kc in range(4):
                                  P.inc("mm", e.matmul(
                                      pf[:, hv * 512:(hv + 1) * 512],
                                      cn3_bf[:, kc * 256 + oc * 128:kc * 256 + (oc + 1) * 128],
                                      u2[:, kc * TC + hv * 512:kc * TC + (hv + 1) * 512],
                                      start=(kc == 0), stop=(kc == 3)), 1)
                      t_f3 = P.val("mm")

                      @blk.scalar
                      def _(e, oc=oc, pf=pf, t_f3=t_f3):
                          P.wait(e, "mm", t_f3)
                          P.inc("act", e.activation(u3[:, oc * TC:(oc + 1) * TC], pf,
                                                    AF.Gelu, bias=cnb3[:, oc:oc + 1]), 1)
                  t_u3 = P.val("act")

                  @blk.tensor
                  def _(e, t_u3=t_u3):
                      P.wait(e, "act", t_u3)
                      for hv in range(2):
                          for kc in range(2):
                              P.inc("mm", e.matmul(
                                  pf4[:, hv * 512:(hv + 1) * 512], cn4_bf[:, kc:kc + 1],
                                  u3[:, kc * TC + hv * 512:kc * TC + (hv + 1) * 512],
                                  start=(kc == 0), stop=(kc == 1)), 1)
                  t_f4 = P.val("mm")

                  @blk.scalar
                  def _(e, t_f4=t_f4):
                      P.wait(e, "mm", t_f4)
                      P.inc("act", e.activation(tf, pf4, AF.Sigmoid, bias=cnb4), 1)
                  t_tf = P.val("act")

                  @blk.vector
                  def _(e, t_tf=t_tf):
                      P.wait(e, "act", t_tf)
                      P.inc("dve", e.tensor_scalar(coup, tf, gf_sb[0:1, 1:2],
                                                   gf_sb[0:1, 2:3], ALU.mult, ALU.add), 1)
                  t_cp = P.val("dve")

                  @blk.tensor
                  def _(e, t_cp=t_cp):
                      P.wait(e, "dve", t_cp)
                      for hv in range(2):
                          P.inc("mm", e.matmul(
                              pcr[:, hv * 512:(hv + 1) * 512], ones_row,
                              coup[:, hv * 512:(hv + 1) * 512],
                              start=True, stop=True), 1)
                  t_cpr = P.val("mm")

                  @blk.vector
                  def _(e, t_cpr=t_cpr):
                      P.wait(e, "mm", t_cpr)
                      P.inc("dve", e.tensor_copy(cpr[:, 0:TC], pcr), 1)
                      P.wait(e, "dve", P.val("dve"))
                      P.inc("dve", e.tensor_scalar_mul(cpr[:, TC:2 * TC], cpr[:, 0:TC], -1.0), 1)
                  t_cprd = P.val("dve")

                  @blk.vector
                  def _(e, cur=cur, nxt=nxt, t_cprd=t_cprd):
                      P.wait(e, "dve", t_cprd)
                      for oc in range(8):
                          so = TW_SRC[oc]
                          sg = 0 if TW_SGN[oc] > 0 else 1
                          P.inc("dve", e.tensor_mul(
                              nxt[:, oc * TC:(oc + 1) * TC],
                              cur[:, so * TC:(so + 1) * TC],
                              cpr[:, sg * TC:(sg + 1) * TC]), 1)
                          P.wait(e, "dve", P.val("dve"))
                          P.inc("dve", e.tensor_add(
                              nxt[:, oc * TC:(oc + 1) * TC],
                              nxt[:, oc * TC:(oc + 1) * TC],
                              cur[:, oc * TC:(oc + 1) * TC]), 1)
                  t_tw = P.val("dve")
                  cur, nxt = nxt, cur
                  gate = ("dve", t_tw)

              for tt in range(TC // 128):
                  ti = tc3 * (TC // 128) + tt
                  sl = tt % 2

                  @blk.sync
                  def _(e, ti=ti, sl=sl, t_tw=t_tw):
                      if ti == 0:
                          P.wait2(e, xfer["x1_last"])
                      if ti >= 2:
                          P.wait(e, "dve", x1r_free[ti - 2])
                      P.inc("d38", e.dma_start(
                          out=x1r[:, sl * H:(sl + 1) * H],
                          in_=x1_sp.ap()[ti * 128:(ti + 1) * 128, :]), 16)
                  t_x1l = P.tval("d38")
                  if tc3 == 0 and tt == 0:
                      x1r_free = {}
                      x2_written = {}

                  for kc in range(NKC):
                      @blk.tensor
                      def _(e, kc=kc, tt=tt, cur=cur, t_tw=t_tw):
                          P.wait(e, "dve", max(t_tw, P.val("dve")))
                          P.inc("mm", e.transpose(
                              ptq.bitcast(BF16)[:, 0:128],
                              cur[:, kc * TC + tt * 128:kc * TC + tt * 128 + 128],
                              ident_bf), 1)
                      t_tq = P.val("mm")

                      @blk.vector
                      def _(e, kc=kc, t_tq=t_tq):
                          P.wait(e, "mm", t_tq)
                          P.inc("dve", e.tensor_copy(
                              ffT[:, kc * 128:(kc + 1) * 128],
                              ptq.bitcast(BF16)[:, 0:128]), 1)
                  t_ffT = P.val("dve")

                  @blk.vector
                  def _(e, sl=sl, ti=ti, t_ffT=t_ffT, t_x1l=t_x1l):
                      P.wait2(e, t_x1l)
                      P.inc("dve", e.tensor_mul(ffT, ffT, gb_rep), 1)
                      if ti > 0:
                          P.wait2(e, x2_written[ti - 1])
                      P.wait(e, "dve", P.val("dve"))
                      P.inc("dve", e.tensor_add(x2c, x1r[:, sl * H:(sl + 1) * H], ffT), 1)
                      x1r_free[ti] = P.val("dve")
                  t_x2 = P.val("dve")

                  @blk.gpsimd
                  def _(e, ti=ti, t_x2=t_x2):
                      P.wait(e, "dve", t_x2)
                      P.inc("d39", e.dma_start(
                          out=out_h.ap()[ti * 128:(ti + 1) * 128, :], in_=x2c), 16)
                      x2_written[ti] = P.tval("d39")
              out_written[tc3] = P.tval("d39")

      _st.close()
      return P.tval("d39")

    t_final = _phase3b()

    @blk.gpsimd
    def _(e):
        P.wait2(e, t_final)

    blk_cm.__exit__(None, None, None)
    return nc, taps


_BUILD_CACHE = {}


def _get_nc():
    if "nc" not in _BUILD_CACHE:
        _BUILD_CACHE["nc"] = build()
    return _BUILD_CACHE["nc"]


def kernel(**inputs):
    f32 = np.float32
    g = lambda k: np.ascontiguousarray(np.asarray(inputs[k], dtype=f32))

    x = g("L0_attn_x")
    W_in = g("L0_attn_W_in")
    conv_w = g("L0_attn_conv_w")[:, 0, :]
    conv_b = g("L0_attn_conv_b")
    W_x = g("L0_attn_W_x")
    W_dt = g("L0_attn_W_dt")
    b_dt = g("L0_attn_b_dt")
    A_log = g("L0_attn_A_log")
    W_out = g("L0_attn_W_out")
    cvec = g("L0_c")
    wmod = g("L0_wmod")

    nc, _ = _get_nc()
    nonce = int(np.random.randint(1, 2 ** 30))

    in_maps = []
    for core in range(8):
        b, h = core // 2, core % 2
        dsl = slice(h * DH, (h + 1) * DH)

        win_t = np.concatenate(
            [W_in[h * DH:(h + 1) * DH, :].T,
             W_in[D2 + h * DH:D2 + (h + 1) * DH, :].T], axis=1)
        wx_t = np.zeros((DH, 128), f32)
        wx_t[:, 0:64] = W_x[0:64, dsl].T
        wx_t[:, 64:80] = W_x[64:80, dsl].T
        wx_t[:, 96:112] = W_x[80:96, dsl].T
        wdt_t = W_dt[dsl, :].T
        wout_t = W_out[:, dsl].T
        cw = conv_w[dsl].reshape(NDBLK, 128, 4).transpose(1, 0, 2).reshape(128, NDBLK * 4)
        cb = conv_b[dsl].reshape(NDBLK, 128).T
        bd = b_dt[dsl].reshape(NDBLK, 128).T
        a_m = A_log[dsl, :].reshape(NDBLK, 128, N).transpose(1, 0, 2) \
            .reshape(128, NDBLK * N)

        def colmaj(v, nb):
            return np.asarray(v, f32).reshape(nb, 128).T

        m = dict(
            x=np.ascontiguousarray(x[b]),
            x_own=np.ascontiguousarray(x[b, h * THALF:(h + 1) * THALF, :]),
            win_t=np.ascontiguousarray(win_t),
            wout_t=np.ascontiguousarray(wout_t),
            wx_t=np.ascontiguousarray(wx_t),
            wdt_t=np.ascontiguousarray(wdt_t),
            conv_w=np.ascontiguousarray(cw),
            conv_b=np.ascontiguousarray(cb),
            b_dt=np.ascontiguousarray(bd),
            a_log=np.ascontiguousarray(a_m),
            c_row=np.ascontiguousarray(cvec[b:b + 1]),
            wmod=np.ascontiguousarray(wmod),
            cn_w1=np.ascontiguousarray(g("L0_ffn_cn_w1")),
            cn_b1=colmaj(g("L0_ffn_cn_b1"), 8),
            cn_w2=np.ascontiguousarray(g("L0_ffn_cn_w2")),
            cn_b2=colmaj(g("L0_ffn_cn_b2"), 4),
            cn_w3=np.ascontiguousarray(g("L0_ffn_cn_w3")),
            cn_b3=colmaj(g("L0_ffn_cn_b3"), 2),
            cn_w4=np.ascontiguousarray(g("L0_ffn_cn_w4").reshape(2, 128).T),
            cn_b4=np.asarray(g("L0_ffn_cn_b4")).reshape(1, 1),
            gc_w1=np.ascontiguousarray(g("L0_ffn_gc_w1")),
            gc_b1=colmaj(g("L0_ffn_gc_b1"), 4),
            gc_w2=np.ascontiguousarray(g("L0_ffn_gc_w2")),
            gc_b2=colmaj(g("L0_ffn_gc_b2"), 2),
            gc_w3=np.ascontiguousarray(g("L0_ffn_gc_w3").reshape(2, 128).T),
            gc_b3=np.asarray(g("L0_ffn_gc_b3")).reshape(1, 1),
            ar=np.asarray(g("L0_ffn_adaptive_range")).reshape(1, 1),
            meta=np.array([[h, 1 - h, nonce + 1, nonce + 2, nonce + 3,
                            h * THALF, 0, 0]], np.int32),
        )
        in_maps.append(m)

    res = run_bass_kernel_spmd(nc, in_maps, core_ids=list(range(8)))
    out = np.empty((B, L, H), f32)
    for core in range(8):
        b, h = core // 2, core % 2
        out[b, h * THALF:(h + 1) * THALF, :] = res.results[core]["out"]
    return out


# version 08 (final)
# speedup vs baseline: 1.0045x; 1.0045x over previous
import contextlib
import numpy as np
import ml_dtypes
import concourse.bass as bass
import concourse.mybir as mybir
from concourse.bass_utils import run_bass_kernel_spmd
from concourse.expressions_rust import RuntimeValue

F32 = mybir.dt.float32
BF16 = mybir.dt.bfloat16
I32 = mybir.dt.int32
AF = mybir.ActivationFunctionType
ALU = mybir.AluOpType
AX = mybir.AxisListType

B, L, H = 4, 8192, 1024
D2, N, RNK = 2048, 16, 64
DH = D2 // 2
NDBLK = DH // 128
NKC = H // 128
THALF = L // 2
TC = 1024
NTC = L // TC
TCP1 = 512
NTCP1 = L // TCP1
TC2 = 512
NTC2 = L // TC2
AP1 = TCP1 // 128
NTC3 = THALF // TC
EPS = 1e-5


class Prog:

    def __init__(self, nc, blk):
        self.nc = nc
        self.blk = blk
        self._sems = {}

    def sem(self, name):
        if name not in self._sems:
            cm = self.nc.semaphore(name)
            h = cm.__enter__()
            self._sems[name] = [h, 0, cm]
        return self._sems[name]

    def inc(self, name, instr, n):
        s = self.sem(name)
        instr.then_inc(s[0], n)
        s[1] += n
        return s[1]

    def val(self, name):
        return self.sem(name)[1]

    def tval(self, name):
        return (name, self.sem(name)[1])

    def wait(self, eng, name, v):
        eng.wait_ge(self.sem(name)[0], v)

    def wait2(self, eng, tv):
        eng.wait_ge(self.sem(tv[0])[0], tv[1])


def build(debug_taps=()):
    nc = bass.Bass("TRN2", target_bir_lowering=False,
                   detect_race_conditions=False)

    def din(name, shape, dt=F32):
        return nc.dram_tensor(name, shape, dt, kind="ExternalInput")

    x_h = din("x", [L, H])
    xo_h = din("x_own", [THALF, H])
    win_h = din("win_t", [H, 2048])
    wout_h = din("wout_t", [DH, H])
    wx_h = din("wx_t", [DH, 128])
    wdt_h = din("wdt_t", [RNK, DH])
    convw_h = din("conv_w", [128, NDBLK * 4])
    convb_h = din("conv_b", [128, NDBLK])
    bdt_h = din("b_dt", [128, NDBLK])
    a_h = din("a_log", [128, NDBLK * N])
    c_h = din("c_row", [1, H])
    wmod_h = din("wmod", [H, 6 * H])
    cnw1_h = din("cn_w1", [H, H])
    cnb1_h = din("cn_b1", [128, 8])
    cnw2_h = din("cn_w2", [H, 512])
    cnb2_h = din("cn_b2", [128, 4])
    cnw3_h = din("cn_w3", [512, 256])
    cnb3_h = din("cn_b3", [128, 2])
    cnw4_h = din("cn_w4", [128, 2])
    cnb4_h = din("cn_b4", [1, 1])
    gcw1_h = din("gc_w1", [H, 512])
    gcb1_h = din("gc_b1", [128, 4])
    gcw2_h = din("gc_w2", [512, 256])
    gcb2_h = din("gc_b2", [128, 2])
    gcw3_h = din("gc_w3", [128, 2])
    gcb3_h = din("gc_b3", [1, 1])
    ar_h = din("ar", [1, 1])
    meta_h = din("meta", [1, 8], I32)

    out_h = nc.dram_tensor("out", [THALF, H], F32, kind="ExternalOutput")

    xc_sp = nc.dram_tensor("xc_sp", [128, NTC2, NDBLK, TC2], BF16)
    sr_sp = nc.dram_tensor("sr_sp", [128, NTC2, NDBLK, TC2], BF16)
    bc_sp = nc.dram_tensor("bc_sp", [2 * N, L], BF16)
    mod_sp = nc.dram_tensor("mod_sp", [6 * H], F32)
    x1_sp = nc.dram_tensor("x1_sp", [THALF, H], F32)
    h2_sp = nc.dram_tensor("h2_sp", [NKC, 128, THALF], BF16)
    gcv_sp = nc.dram_tensor("gcv_sp", [H], F32)

    xdbl_part = nc.dram_tensor("xdbl_part", [128, L], F32)
    attn_stage = nc.dram_tensor("attn_stage", [NKC, 128, L], F32)
    attn_in = nc.dram_tensor("attn_in", [2, NKC, 128, THALF], F32)
    sh_xdbl = nc.dram_tensor("sh_xdbl", [2, 128, L], F32, addr_space="Shared")
    sh_attn = nc.dram_tensor("sh_attn", [2, NKC, 128, L], F32, addr_space="Shared")
    sh_sum = nc.dram_tensor("sh_sum", [2, 1, H], F32, addr_space="Shared")
    sh_flag = nc.dram_tensor("sh_flag", [2, 1], I32, addr_space="Shared")

    taps = {}

    def tap(name, shape, dt=F32):
        if name in debug_taps:
            taps[name] = nc.dram_tensor("tap_" + name, shape, dt,
                                        kind="ExternalOutput")
            return taps[name]
        return None

    AL = lambda nm, sh, dt: nc.alloc_sbuf_tensor(nm, sh, dt).ap()
    meta = AL("meta_s", [1, 8], I32)
    wout_bf = AL("wout_bf", [128, NDBLK * H], BF16)
    wx_bf = AL("wx_bf", [128, NDBLK * 128], BF16)
    wdt_bf = AL("wdt_bf", [64, DH], BF16)
    convw = AL("convw_s", [128, NDBLK * 4], F32)
    convb = AL("convb_s", [128, NDBLK], F32)
    bdt = AL("bdt_s", [128, NDBLK], F32)
    a_sb = AL("a_sb", [128, NDBLK * N], F32)
    ident_bf = AL("ident_bf", [128, 128], BF16)
    ident_f32 = AL("ident_f32", [128, 128], F32)
    ones_bf = AL("ones_bf", [128, 1], BF16)
    ones_row = AL("ones_row", [1, 128], BF16)
    state = AL("state_s", [128, NDBLK * N], F32)
    xdbl_bf = AL("xdbl_bf", [128, L], BF16)
    flag_src = AL("flag_src", [1, 1], I32)
    spin_sb = AL("spin_sb", [1, 4], I32)
    spin_sp = nc.dram_tensor("spin_sp", [1, 4], I32)
    flag_rd = AL("flag_rd", [1, 1], I32)
    fence_rd = AL("fence_rd", [1, 4], F32)

    consts_sb = AL("consts_sb", [128, 4], F32)
    CONST_VALS = [1e-5, 1.0, -0.5, -1.0]
    consts_h = nc.inline_tensor(
        np.tile(np.array(CONST_VALS, np.float32), (128, 1)), "consts_c")
    for ci, cv in enumerate(CONST_VALS):
        nc.const_aps.register(F32, cv, consts_sb[:, ci:ci + 1])

    id_np = np.eye(128)
    id_h = nc.inline_tensor(id_np.astype(ml_dtypes.bfloat16), "id_bf_c")
    idf_h = nc.inline_tensor(id_np.astype(np.float32), "id_f32_c")
    ones_h = nc.inline_tensor(np.ones((128, 1), dtype=ml_dtypes.bfloat16), "ones_c")
    onesr_h = nc.inline_tensor(np.ones((1, 128), dtype=ml_dtypes.bfloat16), "onesr_c")

    blk_cm = nc.Block()
    blk = blk_cm.__enter__()
    P = Prog(nc, blk)

    @blk.sync
    def _(e):
        for dst, src in [(meta, meta_h.ap()), (convw, convw_h.ap()),
                         (convb, convb_h.ap()), (bdt, bdt_h.ap()),
                         (a_sb, a_h.ap())]:
            P.inc("d01", e.dma_start(out=dst, in_=src), 16)
        P.inc("d01", e.dma_start(out=consts_sb, in_=consts_h.ap()), 16)
        P.inc("d01", e.dma_start(out=ident_bf, in_=id_h.ap().bitcast(BF16)), 16)
        P.inc("d01", e.dma_start(out=ident_f32, in_=idf_h.ap()), 16)
        P.inc("d01", e.dma_start(out=ones_bf, in_=ones_h.ap().bitcast(BF16)), 16)
        P.inc("d01", e.dma_start(out=ones_row, in_=onesr_h.ap().bitcast(BF16)), 16)
    t_const = P.tval("d01")

    regs = {}
    xfer = {}

    @blk.gpsimd
    def _(e):
        P.wait2(e, t_const)
        r1 = e.alloc_register("g_rp_keep")
        e.load(r1, meta[0:1, 0:1])
        regs["g_rp"] = RuntimeValue(r1, min_val=0, max_val=1)
        r2 = e.alloc_register("g_rq_keep")
        e.load(r2, meta[0:1, 1:2])
        regs["g_rq"] = RuntimeValue(r2, min_val=0, max_val=1)

    @blk.scalar
    def _(e):
        P.wait2(e, t_const)
        r5 = e.alloc_register("a_rp_keep")
        e.load(r5, meta[0:1, 0:1])
        regs["a_rp"] = RuntimeValue(r5, min_val=0, max_val=1)

    @blk.sync
    def _(e):
        P.wait2(e, t_const)
        r3 = e.alloc_register("s_rq_keep")
        e.load(r3, meta[0:1, 1:2])
        regs["s_rq"] = RuntimeValue(r3, min_val=0, max_val=1)
        r4 = e.alloc_register("s_rt_keep")
        e.load(r4, meta[0:1, 5:6])
        regs["s_rt"] = RuntimeValue(r4, min_val=0, max_val=THALF)

    @blk.scalar
    def _(e):
        P.wait2(e, t_const)
        P.inc("act", e.activation(a_sb, a_sb, AF.Exp), 1)
    t_aexp = P.val("act")

    @blk.vector
    def _(e):
        P.inc("dve", e.memset(spin_sb, 0.0), 1)
        P.wait(e, "act", t_aexp)
        P.inc("dve", e.tensor_scalar_mul(a_sb, a_sb, -1.0), 1)
    t_aneg = P.val("dve")

    _st = contextlib.ExitStack()
    stg32_t = _st.enter_context(nc.sbuf_tensor("stg32", [128, NKC * 2048], F32))
    silc_t = _st.enter_context(nc.sbuf_tensor("silc", [1, H], F32))
    czero_t = _st.enter_context(nc.sbuf_tensor("czero", [128, H], F32))
    cbf_t = _st.enter_context(nc.sbuf_tensor("cbf", [128, NKC], BF16))
    wmcbf_t = _st.enter_context(nc.sbuf_tensor("wmcbf", [128, 768 * NKC], BF16))
    modsb_t = _st.enter_context(nc.sbuf_tensor("modsb", [128, 48], F32))
    modtr_t = _st.enter_context(nc.sbuf_tensor("modtr", [48, 128], F32))
    gtmp_t = _st.enter_context(nc.sbuf_tensor("gtmp", [128, 16], F32))
    gsg_t = _st.enter_context(nc.sbuf_tensor("gsg", [128, 16], F32))
    p0a_t = _st.enter_context(nc.psum_tensor("p0a", [128, 128], F32))
    p0d_t = _st.enter_context(nc.psum_tensor("p0d", [128, 48], F32))
    if True:
        stg32 = stg32_t.ap(); silc = silc_t.ap(); czero = czero_t.ap()
        cbf = cbf_t.ap(); wmcbf = wmcbf_t.ap(); modsb = modsb_t.ap()
        modtr = modtr_t.ap(); gtmp = gtmp_t.ap(); gsg = gsg_t.ap()
        p0a = p0a_t.ap(); p0d = p0d_t.ap()

        @blk.sync
        def _(e):
            P.inc("d07", e.dma_start(out=silc, in_=c_h.ap()), 16)
        t_win32 = P.tval("d07")

        @blk.sync
        def _(e):
            P.inc("d08", e.dma_start(
                out=stg32[:, 0:NDBLK * H].rearrange("p (d o) -> p d o", d=NDBLK),
                in_=wout_h.ap().rearrange("(d p) o -> p d o", p=128)), 16)
            P.inc("d08", e.dma_start(
                out=stg32[:, NDBLK * H:NDBLK * H + NDBLK * 128].rearrange("p (d o) -> p d o", d=NDBLK),
                in_=wx_h.ap().rearrange("(d p) o -> p d o", p=128)), 16)
            P.inc("d08", e.dma_start(
                out=stg32[0:64, NDBLK * H + NDBLK * 128:
                          NDBLK * H + NDBLK * 128 + DH],
                in_=wdt_h.ap()), 16)
        t_wincast = P.tval("d08")
        t_w32b = P.tval("d08")

        @blk.vector
        def _(e):
            P.wait2(e, t_w32b)
            P.inc("dve", e.tensor_copy(wout_bf, stg32[:, 0:NDBLK * H]), 1)
            P.inc("dve", e.tensor_copy(
                wx_bf, stg32[:, NDBLK * H:NDBLK * H + NDBLK * 128]), 1)
            P.inc("dve", e.tensor_copy(
                wdt_bf, stg32[0:64, NDBLK * H + NDBLK * 128:
                              NDBLK * H + NDBLK * 128 + DH]), 1)
        t_wcast = P.val("dve")

        @blk.scalar
        def _(e):
            P.wait2(e, t_win32)
            P.inc("act", e.activation(silc, silc, AF.Silu), 1)
        t_silc = P.val("act")

        @blk.vector
        def _(e):
            P.inc("dve", e.memset(czero, 0.0), 1)
            P.wait(e, "act", t_silc)
            P.inc("dve", e.tensor_copy(czero[0:1, :], silc), 1)
        t_cz = P.val("dve")

        for kc in range(NKC):
            @blk.tensor
            def _(e, kc=kc):
                if kc == 0:
                    P.wait(e, "dve", t_cz)
                    P.wait2(e, t_const)
                else:
                    P.wait(e, "dve", P.val("dve"))
                P.inc("mm", e.transpose(
                    p0a, czero[:, kc * 128:(kc + 1) * 128], ident_f32), 1)
            t_tp = P.val("mm")

            @blk.vector
            def _(e, kc=kc, t_tp=t_tp):
                P.wait(e, "mm", t_tp)
                P.inc("dve", e.tensor_copy(cbf[:, kc:kc + 1], p0a[:, 0:1]), 1)
        t_cbf = P.val("dve")

        for cc in range(8):
            @blk.sync
            def _(e, cc=cc):
                if cc == 0:
                    P.wait(e, "dve", t_wcast)
                else:
                    P.wait(e, "dve", P.val("dve"))
                P.inc("d11", e.dma_start(
                    out=stg32[:, 0:NKC * 768].rearrange("p (kc o) -> p kc o", kc=NKC),
                    in_=wmod_h.ap()[:, cc * 768:(cc + 1) * 768]
                        .rearrange("(kc p) o -> p kc o", p=128)), 16)
            t_wm = P.tval("d11")

            @blk.vector
            def _(e, t_wm=t_wm, cc=cc):
                P.wait2(e, t_wm)
                if cc > 0:
                    P.wait(e, "mm", P.val("mm"))
                P.inc("dve", e.tensor_copy(wmcbf, stg32[:, 0:NKC * 768]), 1)
            t_wmb = P.val("dve")

            @blk.tensor
            def _(e, cc=cc, t_wmb=t_wmb):
                P.wait(e, "dve", t_wmb)
                if cc == 0:
                    P.wait(e, "dve", t_cbf)
                for ob in range(6):
                    oc = cc * 6 + ob
                    for kc in range(NKC):
                        P.inc("mm", e.matmul(
                            p0d[:, oc:oc + 1],
                            wmcbf[:, kc * 768 + ob * 128:kc * 768 + (ob + 1) * 128],
                            cbf[:, kc:kc + 1],
                            start=(kc == 0), stop=(kc == NKC - 1)), 1)
        t_mod = P.val("mm")

        @blk.vector
        def _(e):
            P.wait(e, "mm", t_mod)
            P.inc("dve", e.tensor_copy(modsb, p0d), 1)
        t_modsb = P.val("dve")

        @blk.scalar
        def _(e):
            P.wait(e, "dve", t_modsb)
            P.inc("act", e.activation(gsg[:, 0:8], modsb[:, 16:24], AF.Sign), 1)
            P.inc("act", e.activation(gsg[:, 8:16], modsb[:, 40:48], AF.Sign), 1)
            P.inc("act", e.activation(gtmp[:, 0:8], modsb[:, 16:24], AF.Abs), 1)
            P.inc("act", e.activation(gtmp[:, 8:16], modsb[:, 40:48], AF.Abs), 1)
        t_g1 = P.val("act")

        @blk.vector
        def _(e):
            P.wait(e, "act", t_g1)
            P.inc("dve", e.tensor_scalar_max(gtmp, gtmp, 0.25), 1)
            P.wait(e, "dve", P.val("dve"))
            P.inc("dve", e.tensor_mul(modsb[:, 16:24], gsg[:, 0:8], gtmp[:, 0:8]), 1)
            P.inc("dve", e.tensor_mul(modsb[:, 40:48], gsg[:, 8:16], gtmp[:, 8:16]), 1)
            P.inc("dve", e.tensor_scalar_add(modsb[:, 8:16], modsb[:, 8:16], 1.0), 1)
            P.inc("dve", e.tensor_scalar_add(modsb[:, 32:40], modsb[:, 32:40], 1.0), 1)
        t_g2 = P.val("dve")

        @blk.tensor
        def _(e):
            P.wait(e, "dve", t_g2)
            P.inc("mm", e.transpose(p0a[0:48, 0:128], modsb, ident_f32), 1)
        t_mtp = P.val("mm")

        @blk.vector
        def _(e):
            P.wait(e, "mm", t_mtp)
            P.inc("dve", e.tensor_copy(modtr, p0a[0:48, 0:128]), 1)
        t_mtr = P.val("dve")

        @blk.gpsimd
        def _(e):
            P.wait(e, "dve", t_mtr)
            P.inc("d12", e.dma_start(
                out=mod_sp.ap().rearrange("(c p) -> c p", p=128), in_=modtr), 16)
        t_modsp = P.tval("d12")

    _st.close()

    sa_rep = AL("sa_rep", [128, H], F32)
    ca1_rep = AL("ca1_rep", [128, H], F32)

    @blk.sync
    def _(e):
        P.wait2(e, t_modsp)
        P.inc("d13", e.dma_start(
            out=sa_rep, in_=mod_sp.ap()[0:H].unsqueeze(0).partition_broadcast(128)), 16)
        P.inc("d13", e.dma_start(
            out=ca1_rep, in_=mod_sp.ap()[H:2 * H].unsqueeze(0).partition_broadcast(128)), 16)
    t_modrep = P.tval("d13")

    spill_tv = {}

    def _phase1():
      _st = contextlib.ExitStack()
      xin_t = _st.enter_context(nc.sbuf_tensor("xin", [128, 2 * AP1 * H], F32))
      winbf_t = _st.enter_context(nc.sbuf_tensor("win_bf", [128, NKC * 2048], BF16))
      lnsc_t = _st.enter_context(nc.sbuf_tensor("lnsc", [128, 16], F32))
      sq1_t = _st.enter_context(nc.sbuf_tensor("sq1", [128, H], F32))
      hmod_t = _st.enter_context(nc.sbuf_tensor("hmod", [128, AP1 * H], BF16))
      hT_t = _st.enter_context(nc.sbuf_tensor("hT", [128, NKC * TCP1], BF16))
      xs1_t = _st.enter_context(nc.sbuf_tensor("xs1", [128, NDBLK * (TCP1 + 4)], BF16))
      xcc_t = _st.enter_context(nc.sbuf_tensor("xcc", [128, NDBLK * TCP1], BF16))
      src1_t = _st.enter_context(nc.sbuf_tensor("src1", [128, NDBLK * TCP1], BF16))
      xdp_t = _st.enter_context(nc.sbuf_tensor("xdp", [128, TCP1], F32))
      ptp_t = _st.enter_context(nc.psum_tensor("ptp", [128, 128], BF16))
      pwa_t = _st.enter_context(nc.psum_tensor("pwa", [128, TCP1], F32))
      pwb_t = _st.enter_context(nc.psum_tensor("pwb", [128, TCP1], F32))
      pxd_t = _st.enter_context(nc.psum_tensor("pxd", [128, TCP1], F32))
      if True:
          xin = xin_t.ap(); win_bf = winbf_t.ap()
          lnsc = lnsc_t.ap(); sq1 = sq1_t.ap()
          hmod = hmod_t.ap(); hT = hT_t.ap(); xs1 = xs1_t.ap()
          xcc = xcc_t.ap(); src1 = src1_t.ap(); xdp = xdp_t.ap()
          ptp = ptp_t.ap(); pwa = pwa_t.ap(); pwb = pwb_t.ap(); pxd = pxd_t.ap()

          for wh in range(2):
              @blk.sync
              def _(e, wh=wh):
                  if wh == 0:
                      P.wait2(e, t_modrep)
                  else:
                      P.wait(e, "dve", P.val("dve"))
                  P.inc("dwin", e.dma_start(
                      out=xin.rearrange("p (kc o) -> p kc o", kc=NKC // 2),
                      in_=win_h.ap()[wh * (H // 2):(wh + 1) * (H // 2), :]
                      .rearrange("(kc p) o -> p kc o", p=128)), 16)
              t_wl = P.tval("dwin")

              @blk.vector
              def _(e, wh=wh, t_wl=t_wl):
                  P.wait2(e, t_wl)
                  P.inc("dve", e.tensor_copy(
                      win_bf[:, wh * (NKC // 2) * 2048:(wh + 1) * (NKC // 2) * 2048],
                      xin[:, 0:(NKC // 2) * 2048]), 1)
          t_winld = P.val("dve")

          @blk.vector
          def _(e):
              P.wait(e, "dve", t_winld)
              for db in range(NDBLK):
                  P.inc("dve", e.memset(xs1[:, db * (TCP1 + 4):db * (TCP1 + 4) + 4], 0.0), 1)
          t_halo0 = P.val("dve")

          xin_free = {}
          hT_free = {}
          xd_free = {}
          xdp_spilled = {}

          for tc in range(NTCP1):
              xb = xin[:, (tc % 2) * AP1 * H:((tc % 2) + 1) * AP1 * H]

              @blk.sync
              def _(e, tc=tc, xb=xb):
                  if tc == 0:
                      P.wait(e, "dve", t_winld)
                  if tc >= 2:
                      P.wait(e, "dve", xin_free[tc - 2])
                  P.inc("d15", e.dma_start(
                      out=xb.rearrange("p (a h) -> p a h", a=AP1),
                      in_=x_h.ap()[tc * TCP1:(tc + 1) * TCP1, :]
                      .rearrange("(a p) h -> p a h", p=128)), 16)
              t_x = P.tval("d15")

              for a in range(AP1):
                  xt = xb[:, a * H:(a + 1) * H]

                  @blk.vector
                  def _(e, xt=xt, a=a, t_x=t_x):
                      if a == 0:
                          P.wait2(e, t_x)
                      P.inc("dve", e.reduce_sum(lnsc[:, 0:1], xt, axis=AX.X), 1)
                  t_rs = P.val("dve")

                  @blk.scalar
                  def _(e, xt=xt, a=a, t_x=t_x, t_rs=t_rs):
                      if a == 0:
                          P.wait2(e, t_x)
                      P.wait(e, "dve", t_rs)
                      P.inc("act", e.activation(sq1, xt, AF.Square,
                                                accum_out=lnsc[:, 1:2]), 1)
                  t_sq = P.val("act")

                  @blk.vector
                  def _(e, t_sq=t_sq):
                      P.wait(e, "act", t_sq)
                      P.inc("dve", e.tensor_scalar_mul(lnsc[:, 2:3], lnsc[:, 0:1], 1.0 / H), 1)
                      P.inc("dve", e.tensor_scalar_mul(lnsc[:, 3:4], lnsc[:, 1:2], 1.0 / H), 1)
                      P.wait(e, "dve", P.val("dve"))
                      P.inc("dve", e.tensor_mul(lnsc[:, 4:5], lnsc[:, 2:3], lnsc[:, 2:3]), 1)
                      P.wait(e, "dve", P.val("dve"))
                      P.inc("dve", e.tensor_sub(lnsc[:, 5:6], lnsc[:, 3:4], lnsc[:, 4:5]), 1)
                  t_var = P.val("dve")

                  @blk.scalar
                  def _(e, t_var=t_var):
                      P.wait(e, "dve", t_var)
                      P.inc("act", e.activation(lnsc[:, 6:7], lnsc[:, 5:6], AF.Ln, bias=EPS), 1)
                      P.inc("act", e.activation(lnsc[:, 7:8], lnsc[:, 6:7], AF.Exp, scale=-0.5), 1)
                  t_rstd = P.val("act")

                  @blk.vector
                  def _(e, xt=xt, a=a, t_rstd=t_rstd, tc=tc):
                      P.wait(e, "act", t_rstd)
                      if a == 0 and tc > 0:
                          P.wait(e, "mm", hmod_mm_done[tc - 1])
                      P.inc("dve", e.tensor_mul(lnsc[:, 8:9], lnsc[:, 2:3], lnsc[:, 7:8]), 1)
                      P.wait(e, "dve", P.val("dve"))
                      P.inc("dve", e.tensor_scalar(sq1, xt, lnsc[:, 7:8], lnsc[:, 8:9],
                                                   ALU.mult, ALU.subtract), 1)
                      P.wait(e, "dve", P.val("dve"))
                      P.inc("dve", e.tensor_mul(sq1, sq1, ca1_rep), 1)
                      P.wait(e, "dve", P.val("dve"))
                      P.inc("dve", e.tensor_add(hmod[:, a * H:(a + 1) * H], sq1, sa_rep), 1)
              t_hmod = P.val("dve")
              xin_free[tc] = t_hmod

              for a in range(AP1):
                  for kc in range(NKC):
                      @blk.tensor
                      def _(e, a=a, kc=kc, t_hmod=t_hmod, tc=tc):
                          if a == 0 and kc == 0:
                              P.wait(e, "dve", t_hmod)
                              if tc > 0:
                                  P.wait(e, "mm", hT_free[tc - 1])
                          else:
                              P.wait(e, "dve", P.val("dve"))
                          P.inc("mm", e.transpose(
                              ptp, hmod[:, a * H + kc * 128:a * H + (kc + 1) * 128],
                              ident_bf), 1)
                      t_tp = P.val("mm")

                      @blk.vector
                      def _(e, a=a, kc=kc, t_tp=t_tp):
                          P.wait(e, "mm", t_tp)
                          P.inc("dve", e.tensor_copy(
                              hT[:, kc * TCP1 + a * 128:kc * TCP1 + (a + 1) * 128], ptp), 1)
              t_hT = P.val("dve")
              if tc == 0:
                  hmod_mm_done = {}
              hmod_mm_done[tc] = P.val("mm")

              for oc in range(16):
                  pw = pwa if oc % 2 == 0 else pwb

                  @blk.tensor
                  def _(e, oc=oc, pw=pw, t_hT=t_hT, tc=tc):
                      if oc == 0:
                          P.wait(e, "dve", t_hT)
                      gk = tc * 16 + oc
                      if gk >= 2:
                          P.wait(e, "evac", evac_done[gk - 2])
                      for kc in range(NKC):
                          P.inc("mm", e.matmul(
                              pw, win_bf[:, kc * 2048 + oc * 128:kc * 2048 + (oc + 1) * 128],
                              hT[:, kc * TCP1:(kc + 1) * TCP1],
                              start=(kc == 0), stop=(kc == NKC - 1)), 1)
                  t_win = P.val("mm")
                  if tc == 0 and oc == 0:
                      evac_done = {}

                  gk = tc * 16 + oc
                  if oc < 8:
                      @blk.vector
                      def _(e, oc=oc, pw=pw, t_win=t_win, gk=gk):
                          P.wait(e, "mm", t_win)
                          ins = e.tensor_copy(
                              xs1[:, oc * (TCP1 + 4) + 4:(oc + 1) * (TCP1 + 4)], pw)
                          evac_done[gk] = P.inc("evac", ins, 1)
                  else:
                      @blk.scalar
                      def _(e, oc=oc, pw=pw, t_win=t_win, gk=gk, tc=tc):
                          P.wait(e, "mm", t_win)
                          if oc == 8 and tc > 0:
                              P.wait2(e, sr_spilled_d[tc - 1])
                          ins = e.activation(src1[:, (oc - 8) * TCP1:(oc - 7) * TCP1],
                                             pw, AF.Silu)
                          evac_done[gk] = P.inc("evac", ins, 1)
              t_evac = P.val("evac")
              t_sr = t_evac
              t_xs = t_evac
              hT_free[tc] = P.val("mm")

              @blk.gpsimd
              def _(e, tc=tc, t_sr=t_sr):
                  P.wait(e, "evac", t_sr)
                  P.inc("d16", e.dma_start(
                      out=sr_sp.ap()[:, tc],
                      in_=src1.rearrange("p (d t) -> p d t", d=NDBLK)), 16)
              if tc == 0:
                  sr_spilled_d = {}
              sr_spilled_d[tc] = P.tval("d16")
              spill_tv["sr"] = sr_spilled_d[tc]

              for db in range(NDBLK):
                  xsd = xs1[:, db * (TCP1 + 4):(db + 1) * (TCP1 + 4)]
                  xcd = xcc[:, db * TCP1:(db + 1) * TCP1]

                  @blk.vector
                  def _(e, db=db, xsd=xsd, xcd=xcd, t_xs=t_xs, tc=tc):
                      if db == 0:
                          P.wait(e, "evac", t_xs)
                          if tc == 0:
                              P.wait2(e, t_const)
                              P.wait(e, "dve", t_halo0)
                          if tc > 0:
                              P.wait2(e, xc_spilled[tc - 1])
                      P.inc("dve", e.tensor_scalar_mul(
                          xcd, xsd[:, 1:1 + TCP1], convw[:, db * 4:db * 4 + 1]), 1)
                      for j in range(1, 4):
                          P.wait(e, "dve", P.val("dve"))
                          P.inc("dve", e.scalar_tensor_tensor(
                              xcd, xsd[:, 1 + j:1 + j + TCP1],
                              convw[:, db * 4 + j:db * 4 + j + 1],
                              xcd, ALU.mult, ALU.add), 1)
                  t_conv = P.val("dve")
                  if tc == 0 and db == 0:
                      xc_spilled = {}

                  @blk.scalar
                  def _(e, db=db, xcd=xcd, t_conv=t_conv):
                      P.wait(e, "dve", t_conv)
                      P.inc("act", e.activation(xcd, xcd, AF.Silu,
                                                bias=convb[:, db:db + 1]), 1)
              t_xc = P.val("act")

              @blk.vector
              def _(e, t_xc=t_xc):
                  P.wait(e, "act", t_xc)
                  for db in range(NDBLK):
                      xsd = xs1[:, db * (TCP1 + 4):(db + 1) * (TCP1 + 4)]
                      P.inc("dve", e.tensor_copy(xsd[:, 1:4], xsd[:, TCP1 + 1:TCP1 + 4]), 1)
              t_halo = P.val("dve")
              xin_free[tc] = t_halo

              @blk.gpsimd
              def _(e, tc=tc, t_xc=t_xc):
                  P.wait(e, "act", t_xc)
                  P.inc("d17", e.dma_start(
                      out=xc_sp.ap()[:, tc],
                      in_=xcc.rearrange("p (d t) -> p d t", d=NDBLK)), 16)
                  xc_spilled[tc] = P.tval("d17")
                  spill_tv["xc"] = xc_spilled[tc]

              @blk.tensor
              def _(e, t_xc=t_xc, tc=tc):
                  P.wait(e, "act", t_xc)
                  if tc > 0:
                      P.wait(e, "dve", xd_free[tc - 1])
                  for db in range(NDBLK):
                      P.inc("mm", e.matmul(
                          pxd, wx_bf[:, db * 128:(db + 1) * 128],
                          xcc[:, db * TCP1:(db + 1) * TCP1],
                          start=(db == 0), stop=(db == NDBLK - 1)), 1)
              t_xd = P.val("mm")

              @blk.vector
              def _(e, t_xd=t_xd, tc=tc):
                  P.wait(e, "mm", t_xd)
                  if tc > 0:
                      P.wait2(e, xdp_spilled[tc - 1])
                  P.inc("dve", e.tensor_copy(xdp, pxd), 1)
                  xd_free[tc] = P.val("dve")
              t_xdev = P.val("dve")

              @blk.gpsimd
              def _(e, tc=tc, t_xdev=t_xdev):
                  P.wait(e, "dve", t_xdev)
                  P.inc("d_sp_xd", e.dma_start(
                      out=xdbl_part.ap()[:, tc * TCP1:(tc + 1) * TCP1],
                      in_=xdp), 16)
                  xdp_spilled[tc] = P.tval("d_sp_xd")

      _st.close()
      return P.tval("d_sp_xd")

    t_p1sp = _phase1()

    @blk.scalar
    def _(e):
        P.wait2(e, t_p1sp)
        P.inc("sgxc", e.dma_start(
            out=sh_xdbl.ap()[bass.ts(regs["a_rp"], 1)].squeeze(0),
            in_=xdbl_part.ap()), 16)
    t_p1cp = P.tval("sgxc")

    def emit_signal(ready_tv, fence_ap, flag_col, sg):
        @blk.gpsimd
        def _(e):
            P.wait2(e, ready_tv)
            rp = regs["g_rp"]
            P.inc(sg, e.dma_start(out=fence_rd, in_=fence_ap(rp)), 16)
            P.wait(e, sg, P.val(sg))
            P.inc(sg, e.dma_start(out=flag_src,
                                  in_=meta_h.ap()[0:1, flag_col:flag_col + 1]), 16)
            P.wait(e, sg, P.val(sg))
            P.inc(sg, e.dma_start(
                out=sh_flag.ap()[bass.ts(rp, 1)].squeeze(0), in_=flag_src), 16)
            P.wait(e, sg, P.val(sg))
        return P.tval(sg)

    def emit_poll(t_sig, flag_col, done_sem, poll_sem):
        @blk.sync
        def _(e):
            P.wait2(e, t_sig)
            rq = regs["s_rq"]
            with e.register("rflag_" + done_sem) as rflag, \
                 e.register("rcond_" + done_sem) as rcond, \
                 e.register("rexp_" + done_sem) as rexp, \
                 e.register("rtgt_" + done_sem) as rtgt:
                e.reg_mov(rexp, 0)
                e.reg_mov(rflag, -(2 ** 31) + 10)
                e.load(rtgt, meta[0:1, flag_col:flag_col + 1])
                e.reg_alu(rcond, rflag, rtgt, ALU.not_equal)
                with e.While(rcond):
                    e.reg_add(rexp, rexp, 16)
                    e.dma_start(out=flag_rd,
                                in_=sh_flag.ap()[bass.ts(rq, 1)].squeeze(0)) \
                        .then_inc(P.sem(poll_sem)[0], 16)
                    e.wait_ge(P.sem(poll_sem)[0], rexp)
                    e.load(rflag, flag_rd[0:1, 0:1])
                    e.reg_alu(rcond, rflag, rtgt, ALU.not_equal)
                e.store(spin_sb[0:1, flag_col - 2:flag_col - 1], rexp)
            P.inc("d40", e.dma_start(out=spin_sp.ap(), in_=spin_sb), 16)
            P.inc(done_sem, e.nop(), 1)
        return P.val(done_sem)

    t_e1sig = emit_signal(
        t_p1cp, lambda rp: sh_xdbl.ap()[bass.ts(rp, 1)].squeeze(0)[0:1, L - 4:L], 2,
        "sg1")
    t_e1 = emit_poll(t_e1sig, 2, "e1", "po1")

    _st = contextlib.ExitStack()
    xda_t = _st.enter_context(nc.sbuf_tensor("xda", [128, L], F32))
    xdb_t = _st.enter_context(nc.sbuf_tensor("xdb", [128, L], F32))
    if True:
        xda = xda_t.ap(); xdb = xdb_t.ap()

        @blk.sync
        def _(e):
            P.wait(e, "e1", t_e1)
            P.inc("d18", e.dma_start(out=xda, in_=sh_xdbl.ap()[0]), 16)
            P.inc("d18", e.dma_start(out=xdb, in_=sh_xdbl.ap()[1]), 16)
        t_xdl = P.tval("d18")

        @blk.vector
        def _(e):
            P.wait2(e, t_xdl)
            P.inc("dve", e.tensor_add(xda, xda, xdb), 1)
            P.wait(e, "dve", P.val("dve"))
            P.inc("dve", e.tensor_copy(xdbl_bf, xda), 1)
        t_xdred = P.val("dve")

        @blk.gpsimd
        def _(e):
            P.wait(e, "dve", t_xdred)
            P.inc("d20", e.dma_start(out=bc_sp.ap()[0:N, :], in_=xdbl_bf[64:80, :]), 16)
            P.inc("d20", e.dma_start(out=bc_sp.ap()[N:2 * N, :], in_=xdbl_bf[96:112, :]), 16)
    _st.close()
    t_bcsp = P.tval("d20")

    def _phase2():
      _st = contextlib.ExitStack()
      zsb_t = _st.enter_context(nc.sbuf_tensor("zsb", [128, TC2], F32))
      znz_t = _st.enter_context(nc.sbuf_tensor("znz", [128, TC2], F32))
      zmn_t = _st.enter_context(nc.sbuf_tensor("zmn", [128, TC2], F32))
      zm_t = _st.enter_context(nc.sbuf_tensor("zm", [128, TC2], F32))
      dlt_t = _st.enter_context(nc.sbuf_tensor("dlt", [128, NDBLK * TC2], BF16))
      dut_t = _st.enter_context(nc.sbuf_tensor("dut", [128, NDBLK * TC2], BF16))
      xcr_t = _st.enter_context(nc.sbuf_tensor("xcr", [128, 2 * NDBLK * TC2], BF16))
      srr_t = _st.enter_context(nc.sbuf_tensor("srr", [128, NDBLK * TC2], BF16))
      bbc_t = _st.enter_context(nc.sbuf_tensor("bbc", [128, N * TC2], BF16))
      cbc_t = _st.enter_context(nc.sbuf_tensor("cbc", [128, N * TC2], BF16))
      dAb_t = _st.enter_context(nc.sbuf_tensor("dAb", [128, 2 * N * TC2], BF16))
      rrb_t = _st.enter_context(nc.sbuf_tensor("rrb", [128, 2 * TC2], BF16))
      hhb_t = _st.enter_context(nc.sbuf_tensor("hhb", [128, 2 * TC2], BF16))
      wwb_t = _st.enter_context(nc.sbuf_tensor("wwb", [128, 2 * TC2], BF16))
      ybf_t = _st.enter_context(nc.sbuf_tensor("ybf", [128, TC2], BF16))
      ysr_t = _st.enter_context(nc.sbuf_tensor("ysr", [128, NDBLK * TC2], BF16))
      atev_t = _st.enter_context(nc.sbuf_tensor("atev", [128, 2 * TC2], F32))
      pz_t = _st.enter_context(nc.psum_tensor("pz", [128, TC2], F32))
      py_t = _st.enter_context(nc.psum_tensor("py", [128, TC2], F32))
      pat_t = _st.enter_context(nc.psum_tensor("pat", [128, TC2], F32))
      pat2_t = _st.enter_context(nc.psum_tensor("pat2", [128, TC2], F32))
      if True:
          zsb = zsb_t.ap(); znz = znz_t.ap(); zmn = zmn_t.ap(); zm = zm_t.ap()
          dlt = dlt_t.ap(); dut = dut_t.ap(); xcr = xcr_t.ap(); srr = srr_t.ap()
          bbc = bbc_t.ap(); cbc = cbc_t.ap()
          dAb = dAb_t.ap(); rrb = rrb_t.ap(); hhb = hhb_t.ap(); wwb = wwb_t.ap()
          ybf = ybf_t.ap(); ysr = ysr_t.ap(); atev = atev_t.ap()
          pz = pz_t.ap(); py = py_t.ap(); pat = pat_t.ap(); pat2 = pat2_t.ap()

          xc_free = {}
          dve_ld_gate = {}
          at_free = {}
          slab_done = {}
          zfree = []
          scan_mm = []

          for tc in range(NTC2):
              xcb = xcr[:, (tc % 2) * NDBLK * TC2:((tc % 2) + 1) * NDBLK * TC2]

              @blk.sync
              def _(e, tc=tc, xcb=xcb):
                  if tc == 0:
                      P.wait2(e, t_bcsp)
                      P.wait2(e, spill_tv["xc"])
                      P.wait2(e, spill_tv["sr"])
                  if tc >= 2:
                      P.wait(e, "dve", xc_free[tc - 2])
                  P.inc("d22", e.dma_start(
                      out=xcb.rearrange("p (d t) -> p d t", d=NDBLK),
                      in_=xc_sp.ap()[:, tc]), 16)
                  if tc >= 1:
                      P.wait(e, "dve", dve_ld_gate[tc - 1])
                  P.inc("d22", e.dma_start(
                      out=srr.rearrange("p (d t) -> p d t", d=NDBLK),
                      in_=sr_sp.ap()[:, tc]), 16)
              t_ld = P.tval("d22")

              @blk.gpsimd
              def _(e, tc=tc):
                  if tc == 0:
                      P.wait2(e, t_bcsp)
                  if tc >= 1:
                      P.wait(e, "dve", dve_ld_gate[tc - 1])
                  P.inc("d23", e.dma_start(
                      out=bbc.rearrange("p (n t) -> p n t", n=N),
                      in_=bc_sp.ap()[0:N, tc * TC2:(tc + 1) * TC2]
                      .unsqueeze(0).partition_broadcast(128)), 16)
                  P.inc("d23", e.dma_start(
                      out=cbc.rearrange("p (n t) -> p n t", n=N),
                      in_=bc_sp.ap()[N:2 * N, tc * TC2:(tc + 1) * TC2]
                      .unsqueeze(0).partition_broadcast(128)), 16)
              t_ldb = P.tval("d23")

              for db in range(NDBLK):
                  @blk.tensor
                  def _(e, db=db, tc=tc):
                      if db == 0 and tc == 0:
                          P.wait(e, "dve", t_xdred)
                      if zfree:
                          P.wait(e, "dve", zfree[-1])
                      P.inc("mm", e.matmul(
                          pz, wdt_bf[:, db * 128:(db + 1) * 128],
                          xdbl_bf[0:64, tc * TC2:(tc + 1) * TC2],
                          start=True, stop=True), 1)
                  t_z = P.val("mm")

                  @blk.vector
                  def _(e, db=db, t_z=t_z, tc=tc):
                      P.wait(e, "mm", t_z)
                      P.inc("dve", e.tensor_scalar_add(zsb, pz, bdt[:, db:db + 1]), 1)
                      zfree.append(P.val("dve"))
                      P.wait(e, "dve", P.val("dve"))
                      P.inc("dve", e.tensor_scalar_mul(znz, zsb, -1.0), 1)
                      P.wait(e, "dve", P.val("dve"))
                      P.inc("dve", e.tensor_tensor(zmn, zsb, znz, ALU.min), 1)
                  t_zmn = P.val("dve")

                  @blk.scalar
                  def _(e, t_zmn=t_zmn):
                      P.wait(e, "dve", t_zmn)
                      P.inc("act", e.activation(zmn, zmn, AF.Exp), 1)
                      P.inc("act", e.activation(zmn, zmn, AF.Ln, bias=1.0), 1)
                  t_lnp = P.val("act")

                  @blk.vector
                  def _(e, db=db, t_lnp=t_lnp, t_ld=t_ld, xcb=xcb, tc=tc):
                      P.wait(e, "act", t_lnp)
                      if db == 0 and tc > 0:
                          P.wait(e, "act", last_dA_act[tc - 1])
                      P.inc("dve", e.tensor_scalar_max(zm, zsb, 0.0), 1)
                      P.wait(e, "dve", P.val("dve"))
                      P.inc("dve", e.tensor_add(dlt[:, db * TC2:(db + 1) * TC2], zm, zmn), 1)
                      if db == 0:
                          P.wait2(e, t_ld)
                      P.wait(e, "dve", P.val("dve"))
                      P.inc("dve", e.tensor_mul(
                          dut[:, db * TC2:(db + 1) * TC2],
                          dlt[:, db * TC2:(db + 1) * TC2],
                          xcb[:, db * TC2:(db + 1) * TC2]), 1)
              t_dlt = P.val("dve")
              if tc == 0:
                  pass

              for db in range(NDBLK):
                  gdb = tc * NDBLK + db
                  dslab = dAb[:, (gdb % 2) * N * TC2:((gdb % 2) + 1) * N * TC2]
                  if tc == 0 and db == 0:
                      dA_slab_free = {}

                  @blk.scalar
                  def _(e, db=db, gdb=gdb, dslab=dslab, t_dlt=t_dlt, tc=tc):
                      if db == 0:
                          P.wait(e, "dve", t_dlt)
                      if gdb == 0:
                          P.wait(e, "dve", t_aneg)
                      if gdb >= 2:
                          P.wait(e, "dve", dA_slab_free[gdb - 2])
                      for n in range(N):
                          it = db * N + n
                          P.inc("act", e.activation(
                              dslab[:, n * TC2:(n + 1) * TC2],
                              dlt[:, db * TC2:(db + 1) * TC2], AF.Exp,
                              scale=a_sb[:, it:it + 1]), 1)
                  t_dAs = P.val("act")

                  for n in range(N):
                      it = db * N + n
                      sl = it % 2
                      rrs = rrb[:, sl * TC2:(sl + 1) * TC2]
                      hhs = hhb[:, sl * TC2:(sl + 1) * TC2]
                      wws = wwb[:, sl * TC2:(sl + 1) * TC2]
                      col = it

                      @blk.vector
                      def _(e, db=db, n=n, dslab=dslab, rrs=rrs, hhs=hhs,
                            wws=wws, col=col, t_dAs=t_dAs, tc=tc, it=it):
                          gi = tc * 128 + it
                          if gi >= 2 and scan_mm:
                              P.wait(e, "mm", scan_mm[gi - 2])
                          if db == 0 and n == 0:
                              P.wait2(e, t_ldb)
                          P.inc("dve", e.tensor_mul(
                              rrs, dut[:, db * TC2:(db + 1) * TC2],
                              bbc[:, n * TC2:(n + 1) * TC2]), 1)
                          if n == 0:
                              P.wait(e, "act", t_dAs)
                          init = 0.0 if tc == 0 else state[:, col:col + 1]
                          P.inc("dve", e.tensor_tensor_scan(
                              hhs, dslab[:, n * TC2:(n + 1) * TC2], rrs, init,
                              ALU.mult, ALU.add), 1)
                          P.wait(e, "dve", P.val("dve"))
                          P.inc("dve", e.tensor_copy(state[:, col:col + 1],
                                                     hhs[:, TC2 - 1:TC2]), 1)
                          P.inc("dve", e.tensor_mul(
                              wws, hhs, cbc[:, n * TC2:(n + 1) * TC2]), 1)
                      t_w = P.val("dve")

                      @blk.tensor
                      def _(e, n=n, wws=wws, t_w=t_w, db=db):
                          P.wait(e, "dve", t_w)
                          if n == 0:
                              P.wait(e, "act", P.val("act"))
                          P.inc("mm", e.matmul(py, ident_bf, wws,
                                               start=(n == 0), stop=(n == N - 1)), 1)
                          scan_mm.append(P.val("mm"))
                  dA_slab_free[gdb] = P.val("dve")
                  t_y = P.val("mm")

                  @blk.scalar
                  def _(e, t_y=t_y):
                      P.wait(e, "mm", t_y)
                      P.inc("act", e.activation(ybf, py, AF.Copy), 1)
                  t_ybf = P.val("act")

                  @blk.vector
                  def _(e, db=db, t_ybf=t_ybf, tc=tc):
                      P.wait(e, "act", t_ybf)
                      if tc > 0 and db == 0:
                          P.wait(e, "mm", at_done[tc - 1])
                      P.inc("dve", e.tensor_mul(
                          ysr[:, db * TC2:(db + 1) * TC2], ybf,
                          srr[:, db * TC2:(db + 1) * TC2]), 1)
              t_ysr = P.val("dve")
              xc_free[tc] = t_ysr
              dve_ld_gate[tc] = t_ysr
              if tc == 0:
                  last_dA_act = {}
              last_dA_act[tc] = P.val("act")
              if tc == 0:
                  at_done = {}

              for hc in range(NKC):
                  pp = pat if hc % 2 == 0 else pat2
                  asl = hc % 2

                  @blk.tensor
                  def _(e, hc=hc, pp=pp, t_ysr=t_ysr, tc=tc):
                      if hc == 0:
                          P.wait(e, "dve", t_ysr)
                      if tc * NKC + hc >= 2:
                          P.wait2(e, at_free[tc * NKC + hc - 2])
                      for db in range(NDBLK):
                          P.inc("mm", e.matmul(
                              pp, wout_bf[:, db * H + hc * 128:db * H + (hc + 1) * 128],
                              ysr[:, db * TC2:(db + 1) * TC2],
                              start=(db == 0), stop=(db == NDBLK - 1)), 1)
                  t_at = P.val("mm")
                  if tc == NTC2 - 1 and hc == NKC - 1:
                      at_done[tc] = t_at
                  if hc == NKC - 1:
                      at_done[tc] = t_at

                  @blk.vector
                  def _(e, pp=pp, asl=asl, t_at=t_at):
                      P.wait(e, "mm", t_at)
                      P.inc("dve", e.tensor_copy(atev[:, asl * TC2:(asl + 1) * TC2], pp), 1)
                  t_atev = P.val("dve")

                  @blk.gpsimd
                  def _(e, hc=hc, tc=tc, asl=asl, t_atev=t_atev):
                      P.wait(e, "dve", t_atev)
                      P.inc("spat%d" % asl, e.dma_start(
                          out=attn_stage.ap()[hc, :, tc * TC2:(tc + 1) * TC2],
                          in_=atev[:, asl * TC2:(asl + 1) * TC2]), 16)
                      at_free[tc * NKC + hc] = P.tval("spat%d" % asl)

          @blk.scalar
          def _(e):
              P.wait2(e, at_free[(NTC2 - 1) * NKC + NKC - 2])
              P.wait2(e, at_free[(NTC2 - 1) * NKC + NKC - 1])
              P.inc("slab", e.dma_start(
                  out=sh_attn.ap()[bass.ts(regs["a_rp"], 1)].squeeze(0),
                  in_=attn_stage.ap()), 16)
              slab_done[0] = P.tval("slab")

      _st.close()
      return slab_done[0]

    t_p2sp = _phase2()

    t_e2sig = emit_signal(
        t_p2sp,
        lambda rp: sh_attn.ap()[bass.ts(rp, 1)].squeeze(0)[NKC - 1, 0:1, L - 4:L], 3,
        "sg2")
    t_e2 = emit_poll(t_e2sig, 3, "e2", "po2")

    @blk.sync
    def _(e):
        P.wait(e, "e2", t_e2)
        rt = regs["s_rt"]
        for s in range(2):
            P.inc("d26", e.dma_start(
                out=attn_in.ap()[s],
                in_=sh_attn.ap()[s][:, :, bass.DynSlice(rt, THALF)]), 16)
    t_gather = P.tval("d26")

    ga_rep = AL("ga_rep", [128, H], F32)
    sb_rep = AL("sb_rep", [128, H], F32)
    cb1_rep = AL("cb1_rep", [128, H], F32)
    gb_rep = AL("gb_rep", [128, H], F32)

    @blk.sync
    def _(e):
        P.wait(e, "e2", t_e2)
        P.wait2(e, t_gather)
        for i, dst in enumerate([ga_rep, sb_rep, cb1_rep, gb_rep]):
            seg = [2, 3, 4, 5][i]
            P.inc("d27", e.dma_start(
                out=dst, in_=mod_sp.ap()[seg * H:(seg + 1) * H]
                .unsqueeze(0).partition_broadcast(128)), 16)
    t_p3rep = P.tval("d27")

    def _phase3a():
      _st = contextlib.ExitStack()
      a0t_t = _st.enter_context(nc.sbuf_tensor("a0t", [128, 2 * TC], F32))
      a1t_t = _st.enter_context(nc.sbuf_tensor("a1t", [128, 2 * TC], F32))
      x3_t = _st.enter_context(nc.sbuf_tensor("x3", [128, 2 * H], F32))
      atT_t = _st.enter_context(nc.sbuf_tensor("at_T", [128, H], F32))
      x1c_t = _st.enter_context(nc.sbuf_tensor("x1c", [128, H], F32))
      lnsc3_t = _st.enter_context(nc.sbuf_tensor("lnsc3", [128, 16], F32))
      sq3_t = _st.enter_context(nc.sbuf_tensor("sq3", [128, H], F32))
      h2tm_t = _st.enter_context(nc.sbuf_tensor("h2tm", [128, H], BF16))
      h2T_t = _st.enter_context(nc.sbuf_tensor("h2T", [128, NKC * 128], BF16))
      ptp3_t = _st.enter_context(nc.psum_tensor("ptp3", [128, 128], F32))
      ptp3b_t = _st.enter_context(nc.psum_tensor("ptp3b", [128, 128], BF16))
      psum3_t = _st.enter_context(nc.psum_tensor("psum3", [1, H], F32))
      if True:
          a0t = a0t_t.ap(); a1t = a1t_t.ap(); x3 = x3_t.ap(); atT = atT_t.ap()
          x1c = x1c_t.ap(); lnsc3 = lnsc3_t.ap(); sq3 = sq3_t.ap()
          h2tm = h2tm_t.ap(); h2T = h2T_t.ap()
          ptp3 = ptp3_t.ap(); ptp3b = ptp3b_t.ap(); psum3 = psum3_t.ap()

          a_ld_free = {}
          x1_spilled = {}
          h2_spilled = {}
          for ti in range(THALF // 128):
              sl = ti % 2

              @blk.sync
              def _(e, ti=ti, sl=sl):
                  if ti == 0:
                      P.wait2(e, t_p3rep)
                  if ti >= 2:
                      P.wait(e, "dve", a_ld_free[ti - 2])
                  P.inc("d28", e.dma_start(
                      out=a0t[:, sl * TC:sl * TC + TC].rearrange("p (k t) -> p k t", k=NKC),
                      in_=attn_in.ap()[0].transpose([1, 0, 2])
                      [:, :, ti * 128:(ti + 1) * 128]), 16)
                  P.inc("d28", e.dma_start(
                      out=a1t[:, sl * TC:sl * TC + TC].rearrange("p (k t) -> p k t", k=NKC),
                      in_=attn_in.ap()[1].transpose([1, 0, 2])
                      [:, :, ti * 128:(ti + 1) * 128]), 16)
                  P.inc("d28", e.dma_start(
                      out=x3[:, sl * H:(sl + 1) * H],
                      in_=xo_h.ap()[ti * 128:(ti + 1) * 128, :]), 16)
              t_l3 = P.tval("d28")

              @blk.vector
              def _(e, sl=sl, t_l3=t_l3):
                  P.wait2(e, t_l3)
                  P.inc("dve", e.tensor_add(a0t[:, sl * TC:sl * TC + TC],
                                            a0t[:, sl * TC:sl * TC + TC],
                                            a1t[:, sl * TC:sl * TC + TC]), 1)
              t_aadd = P.val("dve")

              for kc in range(NKC):
                  @blk.tensor
                  def _(e, kc=kc, sl=sl, t_aadd=t_aadd):
                      if kc == 0:
                          P.wait(e, "dve", t_aadd)
                      else:
                          P.wait(e, "dve", P.val("dve"))
                      P.inc("mm", e.transpose(
                          ptp3, a0t[:, sl * TC + kc * 128:sl * TC + (kc + 1) * 128],
                          ident_f32), 1)
                  t_tp3 = P.val("mm")

                  @blk.vector
                  def _(e, kc=kc, t_tp3=t_tp3):
                      P.wait(e, "mm", t_tp3)
                      P.inc("dve", e.tensor_copy(atT[:, kc * 128:(kc + 1) * 128], ptp3), 1)
              t_atT = P.val("dve")

              @blk.vector
              def _(e, sl=sl, t_atT=t_atT, ti=ti):
                  if ti > 0:
                      P.wait2(e, x1_spilled[ti - 1])
                  P.wait(e, "dve", P.val("dve"))
                  P.inc("dve", e.tensor_mul(atT, atT, ga_rep), 1)
                  P.wait(e, "dve", P.val("dve"))
                  P.inc("dve", e.tensor_add(x1c, x3[:, sl * H:(sl + 1) * H], atT), 1)
              t_x1 = P.val("dve")
              a_ld_free[ti] = t_x1

              @blk.gpsimd
              def _(e, ti=ti, t_x1=t_x1):
                  P.wait(e, "dve", t_x1)
                  P.inc("d31", e.dma_start(
                      out=x1_sp.ap()[ti * 128:(ti + 1) * 128, :], in_=x1c), 16)
                  x1_spilled[ti] = P.tval("d31")
              if ti == 0:
                  pass

              @blk.vector
              def _(e):
                  P.wait(e, "dve", P.val("dve"))
                  P.inc("dve", e.reduce_sum(lnsc3[:, 0:1], x1c, axis=AX.X), 1)
              t_rs3 = P.val("dve")

              @blk.scalar
              def _(e, t_rs3=t_rs3):
                  P.wait(e, "dve", t_rs3)
                  P.inc("act", e.activation(sq3, x1c, AF.Square,
                                            accum_out=lnsc3[:, 1:2]), 1)
              t_sq3 = P.val("act")

              @blk.vector
              def _(e, t_sq3=t_sq3):
                  P.wait(e, "act", t_sq3)
                  P.inc("dve", e.tensor_scalar_mul(lnsc3[:, 2:3], lnsc3[:, 0:1], 1.0 / H), 1)
                  P.inc("dve", e.tensor_scalar_mul(lnsc3[:, 3:4], lnsc3[:, 1:2], 1.0 / H), 1)
                  P.wait(e, "dve", P.val("dve"))
                  P.inc("dve", e.tensor_mul(lnsc3[:, 4:5], lnsc3[:, 2:3], lnsc3[:, 2:3]), 1)
                  P.wait(e, "dve", P.val("dve"))
                  P.inc("dve", e.tensor_sub(lnsc3[:, 5:6], lnsc3[:, 3:4], lnsc3[:, 4:5]), 1)
              t_var3 = P.val("dve")

              @blk.scalar
              def _(e, t_var3=t_var3):
                  P.wait(e, "dve", t_var3)
                  P.inc("act", e.activation(lnsc3[:, 6:7], lnsc3[:, 5:6], AF.Ln, bias=EPS), 1)
                  P.inc("act", e.activation(lnsc3[:, 7:8], lnsc3[:, 6:7], AF.Exp, scale=-0.5), 1)
              t_rstd3 = P.val("act")

              @blk.vector
              def _(e, t_rstd3=t_rstd3, ti=ti):
                  P.wait(e, "act", t_rstd3)
                  if ti > 0:
                      P.wait(e, "mm", h2_mm_done[ti - 1])
                  P.inc("dve", e.tensor_mul(lnsc3[:, 8:9], lnsc3[:, 2:3], lnsc3[:, 7:8]), 1)
                  P.wait(e, "dve", P.val("dve"))
                  P.inc("dve", e.tensor_scalar(sq3, x1c, lnsc3[:, 7:8], lnsc3[:, 8:9],
                                               ALU.mult, ALU.subtract), 1)
                  P.wait(e, "dve", P.val("dve"))
                  P.inc("dve", e.tensor_mul(sq3, sq3, cb1_rep), 1)
                  P.wait(e, "dve", P.val("dve"))
                  P.inc("dve", e.tensor_add(h2tm, sq3, sb_rep), 1)
              t_h2 = P.val("dve")

              @blk.tensor
              def _(e, ti=ti, t_h2=t_h2):
                  P.wait(e, "dve", t_h2)
                  for hv in range(2):
                      P.inc("mm", e.matmul(
                          psum3[:, hv * 512:(hv + 1) * 512], ones_bf,
                          h2tm[:, hv * 512:(hv + 1) * 512],
                          start=(ti == 0), stop=(ti == THALF // 128 - 1)), 1)
              t_hsum = P.val("mm")

              for kc in range(NKC):
                  @blk.tensor
                  def _(e, kc=kc, t_h2=t_h2):
                      P.wait(e, "dve", P.val("dve"))
                      P.inc("mm", e.transpose(
                          ptp3b, h2tm[:, kc * 128:(kc + 1) * 128], ident_bf), 1)
                  t_tph = P.val("mm")

                  @blk.vector
                  def _(e, kc=kc, t_tph=t_tph, ti=ti):
                      P.wait(e, "mm", t_tph)
                      if kc == 0 and ti > 0:
                          P.wait2(e, h2_spilled[ti - 1])
                      P.inc("dve", e.tensor_copy(h2T[:, kc * 128:(kc + 1) * 128], ptp3b), 1)
              t_h2T = P.val("dve")
              if ti == 0:
                  h2_mm_done = {}
              h2_mm_done[ti] = P.val("mm")

              @blk.gpsimd
              def _(e, ti=ti, t_h2T=t_h2T):
                  P.wait(e, "dve", t_h2T)
                  P.inc("d32", e.dma_start(
                      out=h2_sp.ap()[:, :, ti * 128:(ti + 1) * 128]
                      .rearrange("k p t -> p k t"),
                      in_=h2T.rearrange("p (k t) -> p k t", k=NKC)), 16)
                  h2_spilled[ti] = P.tval("d32")
          t_p3asp = P.tval("d32")
          del h2_mm_done

          hsum_t = _st.enter_context(nc.sbuf_tensor("hsum", [1, H], F32))
          if True:
              hsum = hsum_t.ap()

              @blk.vector
              def _(e):
                  P.wait(e, "mm", t_hsum)
                  P.inc("dve", e.tensor_copy(hsum, psum3), 1)
              t_hsev = P.val("dve")

              @blk.gpsimd
              def _(e):
                  P.wait(e, "dve", t_hsev)
                  rp = regs["g_rp"]
                  P.inc("d_sp_sum", e.dma_start(
                      out=sh_sum.ap()[bass.ts(rp, 1)].squeeze(0), in_=hsum), 16)

      _st.close()
      xfer["x1_last"] = x1_spilled[THALF // 128 - 1]
      xfer["h2_last"] = h2_spilled[THALF // 128 - 1]
      return P.tval("d_sp_sum")

    t_e3w = _phase3a()

    t_e3sig = emit_signal(
        t_e3w, lambda rp: sh_sum.ap()[bass.ts(rp, 1)].squeeze(0)[0:1, H - 4:H], 4,
        "sg3")
    t_e3 = emit_poll(t_e3sig, 4, "e3", "po3")

    cn1_bf = AL("cn1_bf", [128, NKC * H], BF16)
    cn2_bf = AL("cn2_bf", [128, NKC * 512], BF16)
    cn3_bf = AL("cn3_bf", [128, 4 * 256], BF16)
    cn4_bf = AL("cn4_bf", [128, 2], BF16)
    gc1_bf = AL("gc1_bf", [128, NKC * 512], BF16)
    gc2_bf = AL("gc2_bf", [128, 4 * 256], BF16)
    gc3_bf = AL("gc3_bf", [128, 2], BF16)
    cnb1 = AL("cnb1_s", [128, 8], F32)
    cnb2 = AL("cnb2_s", [128, 4], F32)
    cnb3 = AL("cnb3_s", [128, 2], F32)
    cnb4 = AL("cnb4_s", [1, 1], F32)
    gcb1 = AL("gcb1_s", [128, 4], F32)
    gcb2 = AL("gcb2_s", [128, 2], F32)
    gcb3 = AL("gcb3_s", [1, 1], F32)
    arv = AL("arv_s", [1, 1], F32)

    wstg_t = _st.enter_context(nc.sbuf_tensor("wstg", [128, NKC * H], F32))
    if True:
        wstg = wstg_t.ap()
        for (dst, src, cols, kcs) in [
            (cn1_bf, cnw1_h, H, NKC), (cn2_bf, cnw2_h, 512, NKC),
            (cn3_bf, cnw3_h, 256, 4), (cn4_bf, cnw4_h, 2, 1),
            (gc1_bf, gcw1_h, 512, NKC), (gc2_bf, gcw2_h, 256, 4),
            (gc3_bf, gcw3_h, 2, 1),
        ]:
            @blk.sync
            def _(e, dst=dst, src=src, cols=cols, kcs=kcs):
                if dst is cn1_bf:
                    P.wait(e, "e3", t_e3)
                else:
                    P.wait(e, "dve", P.val("dve"))
                P.inc("d33", e.dma_start(
                    out=wstg[:, 0:kcs * cols].rearrange("p (kc o) -> p kc o", kc=kcs),
                    in_=src.ap().rearrange("(kc p) o -> p kc o", p=128)), 16)
            t_ws = P.tval("d33")

            @blk.vector
            def _(e, dst=dst, cols=cols, kcs=kcs, t_ws=t_ws):
                P.wait2(e, t_ws)
                P.inc("dve", e.tensor_copy(dst[:, 0:kcs * cols], wstg[:, 0:kcs * cols]), 1)
        t_ffnw = P.val("dve")
        _st.close()

    @blk.sync
    def _(e):
        for dst, src in [(cnb1, cnb1_h), (cnb2, cnb2_h), (cnb3, cnb3_h),
                         (cnb4, cnb4_h), (gcb1, gcb1_h), (gcb2, gcb2_h),
                         (gcb3, gcb3_h), (arv, ar_h)]:
            P.inc("d34", e.dma_start(out=dst, in_=src.ap()), 16)
    t_bload = P.tval("d34")

    gf_sb = AL("gf_sb", [1, 4], F32)
    _st = contextlib.ExitStack()
    msum_t = _st.enter_context(nc.sbuf_tensor("msum", [1, H], F32))
    mzero_t = _st.enter_context(nc.sbuf_tensor("mzero", [128, H], F32))
    gcfm_t = _st.enter_context(nc.sbuf_tensor("gcfm", [128, NKC], BF16))
    gt1_t = _st.enter_context(nc.sbuf_tensor("gt1", [128, 4], F32))
    gt1b_t = _st.enter_context(nc.sbuf_tensor("gt1b", [128, 4], BF16))
    gt2_t = _st.enter_context(nc.sbuf_tensor("gt2", [128, 2], F32))
    gt2b_t = _st.enter_context(nc.sbuf_tensor("gt2b", [128, 2], BF16))
    pg_t = _st.enter_context(nc.psum_tensor("pg", [128, 4], F32))
    pg1_t = _st.enter_context(nc.psum_tensor("pg1", [1, 1], F32))
    pgt_t = _st.enter_context(nc.psum_tensor("pgt", [128, 128], F32))
    if True:
        msum = msum_t.ap(); mzero = mzero_t.ap(); gcfm = gcfm_t.ap()
        gt1 = gt1_t.ap(); gt1b = gt1b_t.ap(); gt2 = gt2_t.ap(); gt2b = gt2b_t.ap()
        pg = pg_t.ap(); pg1 = pg1_t.ap(); pgt = pgt_t.ap()

        @blk.sync
        def _(e):
            P.wait(e, "e3", t_e3)
            P.inc("d35", e.dma_start(out=msum, in_=sh_sum.ap()[0]), 16)
            P.inc("d35", e.dma_start(out=mzero[0:1, :], in_=sh_sum.ap()[1]), 16)
        t_msl = P.tval("d35")

        @blk.vector
        def _(e):
            P.wait2(e, t_msl)
            P.inc("dve", e.tensor_add(msum, msum, mzero[0:1, :]), 1)
            P.wait(e, "dve", P.val("dve"))
            P.inc("dve", e.tensor_scalar_mul(msum, msum, 1.0 / L), 1)
            P.inc("dve", e.memset(mzero, 0.0), 1)
            P.wait(e, "dve", P.val("dve"))
            P.inc("dve", e.tensor_copy(mzero[0:1, :], msum), 1)
        t_gc0 = P.val("dve")

        for kc in range(NKC):
            @blk.tensor
            def _(e, kc=kc):
                if kc == 0:
                    P.wait(e, "dve", t_gc0)
                else:
                    P.wait(e, "dve", P.val("dve"))
                P.inc("mm", e.transpose(pgt, mzero[:, kc * 128:(kc + 1) * 128],
                                        ident_f32), 1)
            t_gt = P.val("mm")

            @blk.vector
            def _(e, kc=kc, t_gt=t_gt):
                P.wait(e, "mm", t_gt)
                P.inc("dve", e.tensor_copy(gcfm[:, kc:kc + 1], pgt[:, 0:1]), 1)
        t_gcfm = P.val("dve")

        @blk.tensor
        def _(e):
            P.wait(e, "dve", t_gcfm)
            P.wait(e, "dve", t_ffnw)
            for oc in range(4):
                for kc in range(NKC):
                    P.inc("mm", e.matmul(
                        pg[:, oc:oc + 1],
                        gc1_bf[:, kc * 512 + oc * 128:kc * 512 + (oc + 1) * 128],
                        gcfm[:, kc:kc + 1],
                        start=(kc == 0), stop=(kc == NKC - 1)), 1)
        t_g1 = P.val("mm")

        @blk.scalar
        def _(e):
            P.wait(e, "mm", t_g1)
            P.wait2(e, t_bload)
            for oc in range(4):
                P.inc("act", e.activation(gt1[:, oc:oc + 1], pg[:, oc:oc + 1],
                                          AF.Gelu, bias=gcb1[:, oc:oc + 1]), 1)
        t_gt1 = P.val("act")

        @blk.vector
        def _(e):
            P.wait(e, "act", t_gt1)
            P.inc("dve", e.tensor_copy(gt1b, gt1), 1)
        t_gt1b = P.val("dve")

        @blk.tensor
        def _(e):
            P.wait(e, "dve", t_gt1b)
            for oc in range(2):
                for kc in range(4):
                    P.inc("mm", e.matmul(
                        pg[:, oc:oc + 1],
                        gc2_bf[:, kc * 256 + oc * 128:kc * 256 + (oc + 1) * 128],
                        gt1b[:, kc:kc + 1],
                        start=(kc == 0), stop=(kc == 3)), 1)
        t_g2m = P.val("mm")

        @blk.scalar
        def _(e):
            P.wait(e, "mm", t_g2m)
            for oc in range(2):
                P.inc("act", e.activation(gt2[:, oc:oc + 1], pg[:, oc:oc + 1],
                                          AF.Gelu, bias=gcb2[:, oc:oc + 1]), 1)
        t_gt2 = P.val("act")

        @blk.vector
        def _(e):
            P.wait(e, "act", t_gt2)
            P.inc("dve", e.tensor_copy(gt2b, gt2), 1)
        t_gt2b = P.val("dve")

        @blk.tensor
        def _(e):
            P.wait(e, "dve", t_gt2b)
            for kc in range(2):
                P.inc("mm", e.matmul(pg1, gc3_bf[:, kc:kc + 1],
                                     gt2b[:, kc:kc + 1],
                                     start=(kc == 0), stop=(kc == 1)), 1)
        t_g3 = P.val("mm")

        @blk.scalar
        def _(e):
            P.wait(e, "mm", t_g3)
            P.inc("act", e.activation(gf_sb[0:1, 0:1], pg1, AF.Sigmoid,
                                      bias=gcb3), 1)
        t_gf = P.val("act")

        @blk.vector
        def _(e):
            P.wait(e, "act", t_gf)
            P.inc("dve", e.tensor_scalar_mul(gf_sb[0:1, 1:2], arv, 0.6), 1)
            P.inc("dve", e.tensor_mul(gf_sb[0:1, 2:3], arv, gf_sb[0:1, 0:1]), 1)
            P.wait(e, "dve", P.val("dve"))
            P.inc("dve", e.tensor_scalar_mul(gf_sb[0:1, 2:3], gf_sb[0:1, 2:3], 1.4), 1)
            P.wait(e, "dve", P.val("dve"))
            P.inc("dve", e.tensor_sub(gf_sb[0:1, 2:3], gf_sb[0:1, 2:3], arv), 1)
            P.wait(e, "dve", P.val("dve"))
            P.inc("dve", e.tensor_scalar_add(gf_sb[0:1, 2:3], gf_sb[0:1, 2:3], 0.1), 1)
        t_k = P.val("dve")
        _st.close()

    TW_SRC = {0: 4, 1: 5, 2: 6, 3: 7, 4: 0, 5: 1, 6: 2, 7: 3}
    TW_SGN = {0: +1, 1: +1, 2: -1, 3: -1, 4: -1, 5: -1, 6: +1, 7: +1}

    def _phase3b():
      _st = contextlib.ExitStack()
      oA_t = _st.enter_context(nc.sbuf_tensor("oA", [128, NKC * TC], BF16))
      oB_t = _st.enter_context(nc.sbuf_tensor("oB", [128, NKC * TC], BF16))
      u1_t = _st.enter_context(nc.sbuf_tensor("u1", [128, NKC * TC], BF16))
      u2_t = _st.enter_context(nc.sbuf_tensor("u2", [128, 4 * TC], BF16))
      u3_t = _st.enter_context(nc.sbuf_tensor("u3", [128, 2 * TC], BF16))
      tf_t = _st.enter_context(nc.sbuf_tensor("tf", [1, TC], F32))
      coup_t = _st.enter_context(nc.sbuf_tensor("coup", [1, TC], BF16))
      cpr_t = _st.enter_context(nc.sbuf_tensor("cpr", [128, 2 * TC], BF16))
      ffT_t = _st.enter_context(nc.sbuf_tensor("ffT", [128, H], F32))
      x1r_t = _st.enter_context(nc.sbuf_tensor("x1r", [128, 2 * H], F32))
      x2c_t = _st.enter_context(nc.sbuf_tensor("x2c", [128, H], F32))
      pf1_t = _st.enter_context(nc.psum_tensor("pf1", [128, TC], F32))
      pf2_t = _st.enter_context(nc.psum_tensor("pf2", [128, TC], F32))
      pf4_t = _st.enter_context(nc.psum_tensor("pf4", [1, TC], F32))
      pcr_t = _st.enter_context(nc.psum_tensor("pcr", [128, TC], F32))

      if True:
          oA = oA_t.ap(); oB = oB_t.ap(); u1 = u1_t.ap(); u2 = u2_t.ap()
          u3 = u3_t.ap(); tf = tf_t.ap(); coup = coup_t.ap(); cpr = cpr_t.ap()
          ffT = ffT_t.ap(); x1r = x1r_t.ap(); x2c = x2c_t.ap()
          pf1 = pf1_t.ap(); pf2 = pf2_t.ap(); pf4 = pf4_t.ap()
          pcr = pcr_t.ap(); ptq = pcr[:, 0:128]

          for tc3 in range(NTC3):
              @blk.sync
              def _(e, tc3=tc3):
                  P.wait(e, "dve", t_k)
                  if tc3 == 0:
                      P.wait2(e, xfer["h2_last"])
                  if tc3 > 0:
                      P.wait2(e, out_written[tc3 - 1])
                  P.inc("d37", e.dma_start(
                      out=oA.rearrange("p (k t) -> p k t", k=NKC),
                      in_=h2_sp.ap()[:, :, tc3 * TC:(tc3 + 1) * TC]
                      .rearrange("k p t -> p k t")), 16)
              t_h2l = P.tval("d37")
              if tc3 == 0:
                  out_written = {}

              cur, nxt = oA, oB
              gate = t_h2l
              for cyc in range(3):
                  for oc in range(8):
                      pf = pf1 if oc % 2 == 0 else pf2

                      @blk.tensor
                      def _(e, oc=oc, pf=pf, cur=cur, cyc=cyc, tc3=tc3,
                            gate=gate):
                          if oc == 0:
                              P.wait2(e, gate)
                              if tc3 == 0 and cyc == 0:
                                  P.wait(e, "dve", t_ffnw)
                          if oc >= 2:
                              P.wait(e, "act", P.val("act"))
                          for hv in range(2):
                              for kc in range(NKC):
                                  P.inc("mm", e.matmul(
                                      pf[:, hv * 512:(hv + 1) * 512],
                                      cn1_bf[:, kc * H + oc * 128:kc * H + (oc + 1) * 128],
                                      cur[:, kc * TC + hv * 512:kc * TC + (hv + 1) * 512],
                                      start=(kc == 0), stop=(kc == NKC - 1)), 1)
                      t_f1 = P.val("mm")

                      @blk.scalar
                      def _(e, oc=oc, pf=pf, t_f1=t_f1):
                          P.wait(e, "mm", t_f1)
                          P.inc("act", e.activation(u1[:, oc * TC:(oc + 1) * TC], pf,
                                                    AF.Gelu, bias=cnb1[:, oc:oc + 1]), 1)
                  t_u1 = P.val("act")

                  for oc in range(4):
                      pf = pf1 if oc % 2 == 0 else pf2

                      @blk.tensor
                      def _(e, oc=oc, pf=pf, t_u1=t_u1):
                          if oc == 0:
                              P.wait(e, "act", t_u1)
                          else:
                              P.wait(e, "act", P.val("act"))
                          for hv in range(2):
                              for kc in range(NKC):
                                  P.inc("mm", e.matmul(
                                      pf[:, hv * 512:(hv + 1) * 512],
                                      cn2_bf[:, kc * 512 + oc * 128:kc * 512 + (oc + 1) * 128],
                                      u1[:, kc * TC + hv * 512:kc * TC + (hv + 1) * 512],
                                      start=(kc == 0), stop=(kc == NKC - 1)), 1)
                      t_f2 = P.val("mm")

                      @blk.scalar
                      def _(e, oc=oc, pf=pf, t_f2=t_f2):
                          P.wait(e, "mm", t_f2)
                          P.inc("act", e.activation(u2[:, oc * TC:(oc + 1) * TC], pf,
                                                    AF.Gelu, bias=cnb2[:, oc:oc + 1]), 1)
                  t_u2 = P.val("act")

                  for oc in range(2):
                      pf = pf1 if oc % 2 == 0 else pf2

                      @blk.tensor
                      def _(e, oc=oc, pf=pf, t_u2=t_u2):
                          if oc == 0:
                              P.wait(e, "act", t_u2)
                          else:
                              P.wait(e, "act", P.val("act"))
                          for hv in range(2):
                              for kc in range(4):
                                  P.inc("mm", e.matmul(
                                      pf[:, hv * 512:(hv + 1) * 512],
                                      cn3_bf[:, kc * 256 + oc * 128:kc * 256 + (oc + 1) * 128],
                                      u2[:, kc * TC + hv * 512:kc * TC + (hv + 1) * 512],
                                      start=(kc == 0), stop=(kc == 3)), 1)
                      t_f3 = P.val("mm")

                      @blk.scalar
                      def _(e, oc=oc, pf=pf, t_f3=t_f3):
                          P.wait(e, "mm", t_f3)
                          P.inc("act", e.activation(u3[:, oc * TC:(oc + 1) * TC], pf,
                                                    AF.Gelu, bias=cnb3[:, oc:oc + 1]), 1)
                  t_u3 = P.val("act")

                  @blk.tensor
                  def _(e, t_u3=t_u3):
                      P.wait(e, "act", t_u3)
                      for hv in range(2):
                          for kc in range(2):
                              P.inc("mm", e.matmul(
                                  pf4[:, hv * 512:(hv + 1) * 512], cn4_bf[:, kc:kc + 1],
                                  u3[:, kc * TC + hv * 512:kc * TC + (hv + 1) * 512],
                                  start=(kc == 0), stop=(kc == 1)), 1)
                  t_f4 = P.val("mm")

                  @blk.scalar
                  def _(e, t_f4=t_f4):
                      P.wait(e, "mm", t_f4)
                      P.inc("act", e.activation(tf, pf4, AF.Sigmoid, bias=cnb4), 1)
                  t_tf = P.val("act")

                  @blk.vector
                  def _(e, t_tf=t_tf):
                      P.wait(e, "act", t_tf)
                      P.inc("dve", e.tensor_scalar(coup, tf, gf_sb[0:1, 1:2],
                                                   gf_sb[0:1, 2:3], ALU.mult, ALU.add), 1)
                  t_cp = P.val("dve")

                  @blk.tensor
                  def _(e, t_cp=t_cp):
                      P.wait(e, "dve", t_cp)
                      for hv in range(2):
                          P.inc("mm", e.matmul(
                              pcr[:, hv * 512:(hv + 1) * 512], ones_row,
                              coup[:, hv * 512:(hv + 1) * 512],
                              start=True, stop=True), 1)
                  t_cpr = P.val("mm")

                  @blk.vector
                  def _(e, t_cpr=t_cpr):
                      P.wait(e, "mm", t_cpr)
                      P.inc("dve", e.tensor_copy(cpr[:, 0:TC], pcr), 1)
                      P.wait(e, "dve", P.val("dve"))
                      P.inc("dve", e.tensor_scalar_mul(cpr[:, TC:2 * TC], cpr[:, 0:TC], -1.0), 1)
                  t_cprd = P.val("dve")

                  @blk.vector
                  def _(e, cur=cur, nxt=nxt, t_cprd=t_cprd):
                      P.wait(e, "dve", t_cprd)
                      for oc in range(8):
                          so = TW_SRC[oc]
                          sg = 0 if TW_SGN[oc] > 0 else 1
                          P.inc("dve", e.tensor_mul(
                              nxt[:, oc * TC:(oc + 1) * TC],
                              cur[:, so * TC:(so + 1) * TC],
                              cpr[:, sg * TC:(sg + 1) * TC]), 1)
                          P.wait(e, "dve", P.val("dve"))
                          P.inc("dve", e.tensor_add(
                              nxt[:, oc * TC:(oc + 1) * TC],
                              nxt[:, oc * TC:(oc + 1) * TC],
                              cur[:, oc * TC:(oc + 1) * TC]), 1)
                  t_tw = P.val("dve")
                  cur, nxt = nxt, cur
                  gate = ("dve", t_tw)

              for tt in range(TC // 128):
                  ti = tc3 * (TC // 128) + tt
                  sl = tt % 2

                  @blk.sync
                  def _(e, ti=ti, sl=sl, t_tw=t_tw):
                      if ti == 0:
                          P.wait2(e, xfer["x1_last"])
                      if ti >= 2:
                          P.wait(e, "dve", x1r_free[ti - 2])
                      P.inc("d38", e.dma_start(
                          out=x1r[:, sl * H:(sl + 1) * H],
                          in_=x1_sp.ap()[ti * 128:(ti + 1) * 128, :]), 16)
                  t_x1l = P.tval("d38")
                  if tc3 == 0 and tt == 0:
                      x1r_free = {}
                      x2_written = {}

                  for kc in range(NKC):
                      @blk.tensor
                      def _(e, kc=kc, tt=tt, cur=cur, t_tw=t_tw):
                          P.wait(e, "dve", max(t_tw, P.val("dve")))
                          P.inc("mm", e.transpose(
                              ptq.bitcast(BF16)[:, 0:128],
                              cur[:, kc * TC + tt * 128:kc * TC + tt * 128 + 128],
                              ident_bf), 1)
                      t_tq = P.val("mm")

                      @blk.vector
                      def _(e, kc=kc, t_tq=t_tq):
                          P.wait(e, "mm", t_tq)
                          P.inc("dve", e.tensor_copy(
                              ffT[:, kc * 128:(kc + 1) * 128],
                              ptq.bitcast(BF16)[:, 0:128]), 1)
                  t_ffT = P.val("dve")

                  @blk.vector
                  def _(e, sl=sl, ti=ti, t_ffT=t_ffT, t_x1l=t_x1l):
                      P.wait2(e, t_x1l)
                      P.inc("dve", e.tensor_mul(ffT, ffT, gb_rep), 1)
                      if ti > 0:
                          P.wait2(e, x2_written[ti - 1])
                      P.wait(e, "dve", P.val("dve"))
                      P.inc("dve", e.tensor_add(x2c, x1r[:, sl * H:(sl + 1) * H], ffT), 1)
                      x1r_free[ti] = P.val("dve")
                  t_x2 = P.val("dve")

                  @blk.gpsimd
                  def _(e, ti=ti, t_x2=t_x2):
                      P.wait(e, "dve", t_x2)
                      P.inc("d39", e.dma_start(
                          out=out_h.ap()[ti * 128:(ti + 1) * 128, :], in_=x2c), 16)
                      x2_written[ti] = P.tval("d39")
              out_written[tc3] = P.tval("d39")

      _st.close()
      return P.tval("d39")

    t_final = _phase3b()

    @blk.gpsimd
    def _(e):
        P.wait2(e, t_final)

    blk_cm.__exit__(None, None, None)
    return nc, taps


_BUILD_CACHE = {}


def _get_nc():
    if "nc" not in _BUILD_CACHE:
        _BUILD_CACHE["nc"] = build()
    return _BUILD_CACHE["nc"]


def kernel(**inputs):
    f32 = np.float32
    g = lambda k: np.ascontiguousarray(np.asarray(inputs[k], dtype=f32))

    x = g("L0_attn_x")
    W_in = g("L0_attn_W_in")
    conv_w = g("L0_attn_conv_w")[:, 0, :]
    conv_b = g("L0_attn_conv_b")
    W_x = g("L0_attn_W_x")
    W_dt = g("L0_attn_W_dt")
    b_dt = g("L0_attn_b_dt")
    A_log = g("L0_attn_A_log")
    W_out = g("L0_attn_W_out")
    cvec = g("L0_c")
    wmod = g("L0_wmod")

    nc, _ = _get_nc()
    nonce = int(np.random.randint(1, 2 ** 30))

    in_maps = []
    for core in range(8):
        b, h = core // 2, core % 2
        dsl = slice(h * DH, (h + 1) * DH)

        win_t = np.concatenate(
            [W_in[h * DH:(h + 1) * DH, :].T,
             W_in[D2 + h * DH:D2 + (h + 1) * DH, :].T], axis=1)
        wx_t = np.zeros((DH, 128), f32)
        wx_t[:, 0:64] = W_x[0:64, dsl].T
        wx_t[:, 64:80] = W_x[64:80, dsl].T
        wx_t[:, 96:112] = W_x[80:96, dsl].T
        wdt_t = W_dt[dsl, :].T
        wout_t = W_out[:, dsl].T
        cw = conv_w[dsl].reshape(NDBLK, 128, 4).transpose(1, 0, 2).reshape(128, NDBLK * 4)
        cb = conv_b[dsl].reshape(NDBLK, 128).T
        bd = b_dt[dsl].reshape(NDBLK, 128).T
        a_m = A_log[dsl, :].reshape(NDBLK, 128, N).transpose(1, 0, 2) \
            .reshape(128, NDBLK * N)

        def colmaj(v, nb):
            return np.asarray(v, f32).reshape(nb, 128).T

        m = dict(
            x=np.ascontiguousarray(x[b]),
            x_own=np.ascontiguousarray(x[b, h * THALF:(h + 1) * THALF, :]),
            win_t=np.ascontiguousarray(win_t),
            wout_t=np.ascontiguousarray(wout_t),
            wx_t=np.ascontiguousarray(wx_t),
            wdt_t=np.ascontiguousarray(wdt_t),
            conv_w=np.ascontiguousarray(cw),
            conv_b=np.ascontiguousarray(cb),
            b_dt=np.ascontiguousarray(bd),
            a_log=np.ascontiguousarray(a_m),
            c_row=np.ascontiguousarray(cvec[b:b + 1]),
            wmod=np.ascontiguousarray(wmod),
            cn_w1=np.ascontiguousarray(g("L0_ffn_cn_w1")),
            cn_b1=colmaj(g("L0_ffn_cn_b1"), 8),
            cn_w2=np.ascontiguousarray(g("L0_ffn_cn_w2")),
            cn_b2=colmaj(g("L0_ffn_cn_b2"), 4),
            cn_w3=np.ascontiguousarray(g("L0_ffn_cn_w3")),
            cn_b3=colmaj(g("L0_ffn_cn_b3"), 2),
            cn_w4=np.ascontiguousarray(g("L0_ffn_cn_w4").reshape(2, 128).T),
            cn_b4=np.asarray(g("L0_ffn_cn_b4")).reshape(1, 1),
            gc_w1=np.ascontiguousarray(g("L0_ffn_gc_w1")),
            gc_b1=colmaj(g("L0_ffn_gc_b1"), 4),
            gc_w2=np.ascontiguousarray(g("L0_ffn_gc_w2")),
            gc_b2=colmaj(g("L0_ffn_gc_b2"), 2),
            gc_w3=np.ascontiguousarray(g("L0_ffn_gc_w3").reshape(2, 128).T),
            gc_b3=np.asarray(g("L0_ffn_gc_b3")).reshape(1, 1),
            ar=np.asarray(g("L0_ffn_adaptive_range")).reshape(1, 1),
            meta=np.array([[h, 1 - h, nonce + 1, nonce + 2, nonce + 3,
                            h * THALF, 0, 0]], np.int32),
        )
        in_maps.append(m)

    res = run_bass_kernel_spmd(nc, in_maps, core_ids=list(range(8)))
    out = np.empty((B, L, H), f32)
    for core in range(8):
        b, h = core // 2, core % 2
        out[b, h * THALF:(h + 1) * THALF, :] = res.results[core]["out"]
    return out
